# version 03 (final)
# baseline (speedup 1.0000x reference)
import numpy as np
from contextlib import ExitStack

import concourse.bass as bass
import concourse.mybir as mybir
from concourse.bass_utils import run_bass_kernel_spmd
from concourse.aot_env import allow_ici_all_to_all

F32 = mybir.dt.float32
BF16 = mybir.dt.bfloat16
AF = mybir.ActivationFunctionType
ALU = mybir.AluOpType
AX = mybir.AxisListType

NCORES = 8
D = 2048
T = 8192
TOK = T // NCORES
NT = 512
NG = TOK // NT
DKT = D // 128
FF0 = 5632
NFF0 = FF0 // 128
FF1 = 8192
NFF1 = FF1 // 128
NE = 8
EPS = 1e-5

ENGINES = ("tensor", "vector", "scalar", "gpsimd", "sync")


class Prog:
    def __init__(self):
        self.phases = []
        self.new_phase()

    def new_phase(self):
        self.cur = {e: [] for e in ENGINES}
        self.phases.append(self.cur)

    def op(self, eng, fn):
        self.cur[eng].append(("c", fn))

    def dma(self, eng, fn):
        self.cur[eng].append(("d", fn))

    def cc(self, fn):
        self.cur["gpsimd"].append(("cc", fn))


class Fused:

    def __init__(self, nc, st):
        self.nc = nc
        self.st = st
        self.sems = {}
        for e in ENGINES:
            self.sems["c:" + e] = st.enter_context(nc.semaphore("sc_" + e))
            self.sems["d:" + e] = st.enter_context(nc.semaphore("sd_" + e))
        self.cum = {k: 0 for k in self.sems}
        self.ncc = 0
        self.nblk = 0

    def emit_block(self, prog):
        nc = self.nc
        cc_map = {}
        for pi, ph in enumerate(prog.phases):
            for ei, (kind, fn) in enumerate(ph["gpsimd"]):
                if kind == "cc":
                    key = "cc%d" % self.ncc
                    self.ncc += 1
                    cc_map[(pi, ei)] = key
                    self.sems[key] = self.st.enter_context(nc.semaphore(key))
                    self.cum[key] = 0
        base = dict(self.cum)
        for k in self.sems:
            base.setdefault(k, 0)
        totals = []
        cum = dict(base)
        for ph in prog.phases:
            for e in ENGINES:
                for kind, _fn in ph[e]:
                    if kind == "c":
                        cum["c:" + e] += 1
                    elif kind == "d":
                        cum["d:" + e] += 16
            for ei, (kind, _fn) in enumerate(ph["gpsimd"]):
                pass
            totals.append(dict(cum))
        for (pi, ei), key in cc_map.items():
            for pj in range(pi, len(totals)):
                totals[pj][key] = 1
        final = totals[-1] if totals else base
        self.cum = dict(final)

        sems = self.sems

        def body_for(e):
            def body(eng):
                last = {k: 0 for k in sems}
                started = False
                for pi, ph in enumerate(prog.phases):
                    ops = ph[e]
                    if not ops:
                        continue
                    tgt = base if (pi == 0 or not started) else totals[pi - 1]
                    if pi > 0 and started:
                        tgt = totals[pi - 1]
                    else:
                        tgt = dict(base)
                        if pi > 0:
                            tgt.update(totals[pi - 1])
                    for k, v in tgt.items():
                        if v > last.get(k, 0):
                            eng.wait_ge(sems[k], v)
                            last[k] = v
                    started = True
                    for ei, (kind, fn) in enumerate(ops):
                        if kind == "cc":
                            with allow_ici_all_to_all():
                                inst = fn(eng)
                            inst.then_inc(sems[cc_map[(pi, ei)]])
                        else:
                            inst = fn(eng)
                            inst.then_inc(sems[("c:" if kind == "c" else "d:") + e],
                                          1 if kind == "c" else 16)
                for k, v in final.items():
                    if v > last.get(k, 0):
                        eng.wait_ge(sems[k], v)
                        last[k] = v
            return body

        self.nblk += 1
        with self.nc.Block("blk%d" % self.nblk) as blk:
            blk.tensor(body_for("tensor"))
            blk.vector(body_for("vector"))
            blk.scalar(body_for("scalar"))
            blk.gpsimd(body_for("gpsimd"))
            blk.sync(body_for("sync"))


def ln_group(prog, c, y_ap, out_write, nkt=DKT):
    ones = c["ones"]
    s1, s2, pA, pB = c["pb_s1"], c["pb_s2"], c["pb_A"], c["pb_B"]
    stats = c["stats"]
    sq = c["sq"]
    inv_d = 1.0 / (128.0 * nkt)

    for kt in range(nkt):
        prog.op("tensor", lambda e, kt=kt: e.matmul(
            s1[0:1, :], ones[:, 0:1], y_ap(kt), start=(kt == 0), stop=(kt == nkt - 1)))
    prog.new_phase()
    for kt in range(nkt + 1):
        if kt < nkt:
            prog.op("scalar", lambda e, kt=kt: e.activation(
                sq[kt % 2][:, :], y_ap(kt), AF.Square))
        if kt > 0:
            prog.op("tensor", lambda e, kt=kt: e.matmul(
                s2[0:1, :], ones[:, 0:1], sq[(kt - 1) % 2][:, :],
                start=(kt - 1 == 0), stop=(kt - 1 == nkt - 1)))
        prog.new_phase()
    mu = stats[:, 0 * NT:1 * NT]
    ex2 = stats[:, 1 * NT:2 * NT]
    tmp = stats[:, 2 * NT:3 * NT]
    rstd = stats[:, 3 * NT:4 * NT]
    prog.op("scalar", lambda e: e.activation(mu, s1[0:1, :], AF.Identity, scale=inv_d))
    prog.op("vector", lambda e: e.tensor_scalar_mul(ex2, s2[0:1, :], inv_d))
    prog.new_phase()
    prog.op("scalar", lambda e: e.activation(tmp, mu, AF.Square))
    prog.new_phase()
    prog.op("vector", lambda e: e.tensor_sub(ex2, ex2, tmp))
    prog.new_phase()
    prog.op("scalar", lambda e: e.activation(tmp, ex2, AF.Sqrt, bias=c["eps"]))
    prog.new_phase()
    prog.op("vector", lambda e: e.reciprocal(rstd, tmp))
    prog.new_phase()
    prog.op("vector", lambda e: e.tensor_mul(mu, mu, rstd))
    prog.new_phase()
    prog.op("tensor", lambda e: e.matmul(pA[:, :], ones[0:1, :], rstd, start=True, stop=True))
    prog.op("tensor", lambda e: e.matmul(pB[:, :], ones[0:1, :], mu, start=True, stop=True))
    prog.new_phase()
    pending = None
    for kt in range(nkt + 2):
        if pending is not None:
            pending(prog)
            pending = None
        if kt < nkt:
            prog.op("vector", lambda e, kt=kt: e.tensor_mul(sq[kt % 2][:, :], y_ap(kt), pA[:, :]))
        if 0 < kt <= nkt:
            dst, post = out_write(kt - 1)
            prog.op("vector", lambda e, kt=kt, dst=dst: e.tensor_sub(
                dst, sq[(kt - 1) % 2][:, :], pB[:, :]))
            pending = post
        prog.new_phase()


def sec_A(prog, sb, ps, io):
    xr, war, wrr, xvr, r4r = io["xr"], io["war"], io["wrr"], io["xvr"], io["r4r"]
    stage = sb("aA_stage", [128, TOK + 1], F32)
    xbf = sb("aA_xbf", [128, DKT, TOK + 1], BF16)
    wsl = sb("aA_wsl", [128, 32, 128], F32)
    wbf = sb("aA_wbf", [128, 32, 128], BF16)
    outsl = sb("aA_outsl", [128, DKT, TOK], F32)
    pb = [ps("aA_pb%d" % i) for i in range(2)]

    for kt in range(DKT):
        prog.dma("sync", lambda e, kt=kt: e.dma_start(out=stage[:, :], in_=xr[:, kt, :]))
        prog.new_phase()
        prog.op("scalar", lambda e, kt=kt: e.copy(out=xbf[:, kt, :], in_=stage[:, :]))
        prog.new_phase()

    def rhs_for(k, g):
        if k < DKT:
            return xbf[:, k, g * 512:(g + 1) * 512]
        return xbf[:, k - DKT, 1 + g * 512:1 + (g + 1) * 512]

    def wpass(wr_ap, out_r, square_twice):
        for ct in range(DKT):
            for hh in range(2):
                prog.dma("sync", lambda e, ct=ct, hh=hh: e.dma_start(
                    out=wsl[:, hh * 16:(hh + 1) * 16, :],
                    in_=wr_ap[:, hh * 16:(hh + 1) * 16, ct * 128:(ct + 1) * 128]))
            prog.new_phase()
            prog.op("scalar", lambda e: e.copy(
                out=wbf[:, :, :].rearrange("p a b -> p (a b)"),
                in_=wsl[:, :, :].rearrange("p a b -> p (a b)")))
            prog.new_phase()
            for g in range(2):
                for k in range(32):
                    prog.op("tensor", lambda e, g=g, k=k: e.matmul(
                        pb[g][:, :], wbf[:, k, :], rhs_for(k, g),
                        start=(k == 0), stop=(k == 31)))
            prog.new_phase()
            for g in range(2):
                prog.op("scalar", lambda e, ct=ct, g=g: e.activation(
                    outsl[:, ct, g * 512:(g + 1) * 512], pb[g][:, :], AF.Sigmoid))
            prog.new_phase()
            for g in range(2):
                prog.op("vector", lambda e, ct=ct, g=g: e.tensor_scalar_max(
                    outsl[:, ct, g * 512:(g + 1) * 512],
                    outsl[:, ct, g * 512:(g + 1) * 512], 0.5))
            prog.new_phase()
            if square_twice:
                for _ in range(2):
                    prog.op("scalar", lambda e, ct=ct: e.activation(
                        outsl[:, ct, :], outsl[:, ct, :], AF.Square))
                    prog.new_phase()
        for ch in range(4):
            prog.dma("gpsimd", lambda e, ch=ch: e.dma_start(
                out=out_r[:, ch * 4:(ch + 1) * 4, :],
                in_=outsl[:, ch * 4:(ch + 1) * 4, :]))
        prog.new_phase()

    wpass(war, xvr, False)
    wpass(wrr, r4r, True)


def sec_B(prog, sb, ps, io):
    re3, im3, kvre3, kvim3 = io["re3"], io["im3"], io["kvre3"], io["kvim3"]
    rA, iA, rB, iB, tmp = (sb("aB_" + n, [128, T], F32)
                           for n in ("rA", "iA", "rB", "iB", "tmp"))
    v3 = lambda t: t[:, :].rearrange("p (r t2) -> p r t2", r=8)
    prog.dma("sync", lambda e: e.dma_start(out=v3(rA), in_=re3))
    prog.dma("sync", lambda e: e.dma_start(out=v3(iA), in_=im3))
    prog.new_phase()
    prog.op("vector", lambda e: e.tensor_mul(tmp[:, :], rA[:, :], rA[:, :]))
    prog.op("gpsimd", lambda e: e.tensor_mul(rB[:, :], iA[:, :], iA[:, :]))
    prog.new_phase()
    prog.op("vector", lambda e: e.tensor_add(tmp[:, :], tmp[:, :], rB[:, :]))
    prog.new_phase()
    prog.op("scalar", lambda e: e.activation(tmp[:, :], tmp[:, :], AF.Sqrt))
    prog.new_phase()
    prog.op("vector", lambda e: e.tensor_tensor_scan(
        rB[:, :], tmp[:, :], tmp[:, :], 0.0, ALU.max, ALU.bypass))
    prog.new_phase()
    prog.op("vector", lambda e: e.reciprocal(tmp[:, :], rB[:, :]))
    prog.new_phase()
    prog.op("vector", lambda e: e.tensor_mul(rA[:, :], rA[:, :], tmp[:, :]))
    prog.op("gpsimd", lambda e: e.tensor_mul(iA[:, :], iA[:, :], tmp[:, :]))
    prog.new_phase()
    bufs = [rA, iA, rB, iB]
    k = 1
    while k < T:
        ra, ia, rb, ib = bufs
        prog.op("vector", lambda e, k=k, ra=ra, rb=rb: e.tensor_mul(
            rb[:, k:], ra[:, k:], ra[:, :T - k]))
        prog.op("gpsimd", lambda e, k=k, ra=ra, ia=ia, ib=ib: e.tensor_mul(
            ib[:, k:], ra[:, k:], ia[:, :T - k]))
        prog.op("scalar", lambda e, k=k, ra=ra, rb=rb: e.copy(
            out=rb[:, :k], in_=ra[:, :k]))
        prog.new_phase()
        prog.op("vector", lambda e, k=k, ia=ia: e.tensor_mul(
            tmp[:, k:], ia[:, k:], ia[:, :T - k]))
        prog.op("scalar", lambda e, k=k, ia=ia, ib=ib: e.copy(
            out=ib[:, :k], in_=ia[:, :k]))
        prog.new_phase()
        prog.op("vector", lambda e, k=k, rb=rb: e.tensor_sub(
            rb[:, k:], rb[:, k:], tmp[:, k:]))
        prog.new_phase()
        prog.op("vector", lambda e, k=k, ia=ia, ra=ra: e.tensor_mul(
            tmp[:, k:], ia[:, k:], ra[:, :T - k]))
        prog.new_phase()
        prog.op("vector", lambda e, k=k, ib=ib: e.tensor_add(
            ib[:, k:], ib[:, k:], tmp[:, k:]))
        prog.new_phase()
        bufs = [rb, ib, ra, ia]
        k *= 2
    ra, ia = bufs[0], bufs[1]
    prog.dma("gpsimd", lambda e, ra=ra: e.dma_start(out=kvre3, in_=v3(ra)))
    prog.dma("gpsimd", lambda e, ia=ia: e.dma_start(out=kvim3, in_=v3(ia)))
    prog.new_phase()


def sec_C(prog, sb, ps, io):
    xrr, kvr, r4r, wvr, ones_d, x1r = (io["xrr"], io["kvr"], io["r4r"],
                                       io["wvr"], io["ones_d"], io["x1r"])
    xsl = sb("aC_xsl", [128, DKT, TOK], F32)
    r4sl = sb("aC_r4sl", [128, DKT, TOK], F32)
    kvbf = sb("aC_kvbf", [128, DKT, TOK], BF16)
    stage = sb("aC_stage", [128, TOK], F32)
    wsl = sb("aC_wsl", [128, 8, 128], F32)
    wbf = sb("aC_wbf", [128, DKT, 128], BF16)
    scr = sb("aC_scr", [128, 2, 512], F32)
    sq = [sb("aC_sq0", [128, NT], F32), sb("aC_sq1", [128, NT], F32)]
    stats = sb("aC_stats", [1, 4 * NT], F32)
    ones = sb("aC_ones", [128, 128], F32)
    eps = sb("aC_eps", [1, 1], F32)
    pb = [ps("aC_pb%d" % i) for i in range(6)]

    prog.op("vector", lambda e: e.memset(eps[:, :], EPS))
    for ch in range(4):
        prog.dma("sync", lambda e, ch=ch: e.dma_start(
            out=xsl[:, ch * 4:(ch + 1) * 4, :], in_=xrr[:, ch * 4:(ch + 1) * 4, :]))
        prog.dma("sync", lambda e, ch=ch: e.dma_start(
            out=r4sl[:, ch * 4:(ch + 1) * 4, :], in_=r4r[:, ch * 4:(ch + 1) * 4, :]))
    prog.dma("sync", lambda e: e.dma_start(out=ones[:, :], in_=ones_d[:, :]))
    prog.new_phase()
    for kt in range(DKT):
        prog.dma("sync", lambda e, kt=kt: e.dma_start(out=stage[:, :], in_=kvr[:, kt, :]))
        prog.new_phase()
        prog.op("scalar", lambda e, kt=kt: e.copy(out=kvbf[:, kt, :], in_=stage[:, :]))
        prog.new_phase()
    for ct in range(DKT):
        for hh in range(2):
            prog.dma("sync", lambda e, ct=ct, hh=hh: e.dma_start(
                out=wsl[:, :, :],
                in_=wvr[:, hh * 8:(hh + 1) * 8, ct * 128:(ct + 1) * 128]))
            prog.new_phase()
            prog.op("scalar", lambda e, hh=hh: e.copy(
                out=wbf[:, hh * 8:(hh + 1) * 8, :].rearrange("p a b -> p (a b)"),
                in_=wsl[:, :, :].rearrange("p a b -> p (a b)")))
            prog.new_phase()
        for g in range(2):
            for k in range(DKT):
                prog.op("tensor", lambda e, g=g, k=k: e.matmul(
                    pb[g][:, :], wbf[:, k, :], kvbf[:, k, g * 512:(g + 1) * 512],
                    start=(k == 0), stop=(k == DKT - 1)))
        prog.new_phase()
        for g in range(2):
            prog.op("vector", lambda e, ct=ct, g=g: e.tensor_mul(
                scr[:, g, :], pb[g][:, :], r4sl[:, ct, g * 512:(g + 1) * 512]))
        prog.new_phase()
        for g in range(2):
            prog.op("gpsimd", lambda e, ct=ct, g=g: e.tensor_add(
                xsl[:, ct, g * 512:(g + 1) * 512],
                xsl[:, ct, g * 512:(g + 1) * 512], scr[:, g, :]))
        prog.new_phase()
    lnc = {"ones": ones, "pb_s1": pb[2], "pb_s2": pb[3], "pb_A": pb[4], "pb_B": pb[5],
           "stats": stats, "sq": sq, "eps": eps[0:1, 0:1]}
    for g in range(NG):
        def out_write(kt, g=g):
            return r4sl[:, kt, g * NT:(g + 1) * NT], None
        ln_group(prog, lnc, lambda kt, g=g: xsl[:, kt, g * NT:(g + 1) * NT], out_write)
    for ch in range(4):
        prog.dma("gpsimd", lambda e, ch=ch: e.dma_start(
            out=x1r[:, ch * 4:(ch + 1) * 4, :], in_=r4sl[:, ch * 4:(ch + 1) * 4, :]))
    prog.new_phase()


def sec_D1(prog, sb, ps, io):
    x1r, wgr, wur, wdr, gbr, ubr, ybr = (io["x1r"], io["wgr"], io["wur"], io["wdr"],
                                         io["gbr"], io["ubr"], io["ybr"])
    x1bf = sb("aD_x1bf", [128, DKT, TOK], BF16)
    stage = sb("aD_stage", [128, DKT, 128], F32)
    wgbf = sb("aD_wgbf", [128, DKT, 128], BF16)
    wubf = sb("aD_wubf", [128, DKT, 128], BF16)
    wdbf = sb("aD_wdbf", [128, NFF0, 128], BF16)
    base_h = sb("aD_base_h", [128, NFF0, NT], BF16)
    gb = sb("aD_gb", [128, 2, NT], F32)
    ub = sb("aD_ub", [128, 2, NT], F32)
    silu_t = sb("aD_silu", [128, NT], F32)
    xres = sb("aD_xres", [128, NT], F32)
    oscr = sb("aD_oscr", [128, NT], F32)
    pb = [ps("aD_pb%d" % i) for i in range(3)]

    for qq in range(4):
        for half, csl in ((0, slice(0, NT)), (1, slice(NT, TOK))):
            for kk in range(4):
                prog.dma("sync", lambda e, qq=qq, kk=kk, csl=csl: e.dma_start(
                    out=stage[:, kk * 4:(kk + 1) * 4, :].rearrange("p a b -> p (a b)"),
                    in_=x1r[:, qq * 4 + kk, csl]))
            prog.new_phase()
            prog.op("scalar", lambda e, qq=qq, csl=csl: e.copy(
                out=x1bf[:, qq * 4:(qq + 1) * 4, csl],
                in_=stage[:, :, :].rearrange("p (c dd) b -> p c (dd b)", c=4)))
            prog.new_phase()

    for g in range(NG):
        gs = slice(g * NT, (g + 1) * NT)
        for ft in range(NFF0):
            fs = slice(ft * 128, (ft + 1) * 128)
            prog.dma("sync", lambda e, fs=fs: e.dma_start(out=stage[:, :, :],
                                                          in_=wgr[:, :, fs]))
            prog.new_phase()
            prog.op("scalar", lambda e: e.copy(
                out=wgbf[:, :, :].rearrange("p a b -> p (a b)"),
                in_=stage[:, :, :].rearrange("p a b -> p (a b)")))
            prog.new_phase()
            prog.dma("sync", lambda e, fs=fs: e.dma_start(out=stage[:, :, :],
                                                          in_=wur[:, :, fs]))
            for k in range(DKT):
                prog.op("tensor", lambda e, k=k, gs=gs: e.matmul(
                    pb[0][:, :], wgbf[:, k, :], x1bf[:, k, gs],
                    start=(k == 0), stop=(k == DKT - 1)))
            prog.new_phase()
            prog.op("scalar", lambda e: e.copy(
                out=wubf[:, :, :].rearrange("p a b -> p (a b)"),
                in_=stage[:, :, :].rearrange("p a b -> p (a b)")))
            prog.op("vector", lambda e: e.tensor_copy(out=gb[:, 0, :], in_=pb[0][:, :]))
            prog.new_phase()
            for k in range(DKT):
                prog.op("tensor", lambda e, k=k, gs=gs: e.matmul(
                    pb[1][:, :], wubf[:, k, :], x1bf[:, k, gs],
                    start=(k == 0), stop=(k == DKT - 1)))
            prog.op("scalar", lambda e: e.activation(silu_t[:, :], gb[:, 0, :], AF.Silu))
            prog.new_phase()
            prog.op("vector", lambda e: e.tensor_copy(out=ub[:, 0, :], in_=pb[1][:, :]))
            prog.new_phase()
            prog.op("vector", lambda e, ft=ft: e.tensor_mul(
                base_h[:, ft, :], silu_t[:, :], ub[:, 0, :]))
            prog.dma("gpsimd", lambda e, ft=ft, gs=gs: e.dma_start(
                out=gbr[:, ft, gs], in_=gb[:, 0, :]))
            prog.dma("gpsimd", lambda e, ft=ft, gs=gs: e.dma_start(
                out=ubr[:, ft, gs], in_=ub[:, 0, :]))
            prog.new_phase()
        for dt in range(DKT):
            ds = slice(dt * 128, (dt + 1) * 128)
            for ch in range(6):
                ftsl = slice(ch * 8, min((ch + 1) * 8, NFF0))
                n = ftsl.stop - ftsl.start
                prog.dma("sync", lambda e, ds=ds, ftsl=ftsl, n=n: e.dma_start(
                    out=stage[:, 0:n, :], in_=wdr[:, ftsl, ds]))
                prog.new_phase()
                prog.op("scalar", lambda e, ftsl=ftsl, n=n: e.copy(
                    out=wdbf[:, ftsl, :].rearrange("p a b -> p (a b)"),
                    in_=stage[:, 0:n, :].rearrange("p a b -> p (a b)")))
                prog.new_phase()
            prog.dma("sync", lambda e, dt=dt, gs=gs: e.dma_start(
                out=xres[:, :], in_=x1r[:, dt, gs]))
            prog.new_phase()
            for ft in range(NFF0):
                prog.op("tensor", lambda e, ft=ft: e.matmul(
                    pb[2][:, :], wdbf[:, ft, :], base_h[:, ft, :],
                    start=(ft == 0), stop=(ft == NFF0 - 1)))
            prog.new_phase()
            prog.op("vector", lambda e: e.tensor_add(oscr[:, :], pb[2][:, :], xres[:, :]))
            prog.new_phase()
            prog.dma("gpsimd", lambda e, dt=dt, gs=gs: e.dma_start(
                out=ybr[:, dt, gs], in_=oscr[:, :]))
            prog.new_phase()


def sec_D2(prog, sb, ps, io):
    (x1r, gbr, ubr, ybr, wrr, agr, aur, bgr, bur, adr, bdr,
     sel_d, ident_d, ones_d, x2r, bnd) = (
        io["x1r"], io["gbr"], io["ubr"], io["ybr"], io["wrr"], io["agr"], io["aur"],
        io["bgr"], io["bur"], io["adr"], io["bdr"], io["sel_d"], io["ident_d"],
        io["ones_d"], io["x2r"], io.get("bnd"))
    x1bf = sb("a2_x1bf", [128, DKT, NT], BF16)
    ybuf = sb("a2_ybuf", [128, DKT, NT], F32)
    stage = sb("a2_stage", [128, 4, 128], F32)
    wrbf = sb("a2_wrbf", [128, DKT, 8], BF16)
    agbf = sb("a2_agbf", [128, DKT, 16], BF16)
    bst = sb("a2_bst", [16, NE, 128], F32)
    bgbf = sb("a2_bgbf", [16, NE, 128], BF16)
    bubf = sb("a2_bubf", [16, NE, 128], BF16)
    adst = sb("a2_adst", [128, NE, 16], F32)
    adbf = sb("a2_adbf", [128, NE, 16], BF16)
    bdbf = sb("a2_bdbf", [16, NE, 128], BF16)
    tg_bf = sb("a2_tg", [16, NE, NT], BF16)
    tu_bf = sb("a2_tu", [16, NE, NT], BF16)
    s_acc = sb("a2_sacc", [16, NE, NT], F32)
    s_bf = sb("a2_sbf", [16, NE, NT], BF16)
    we8 = sb("a2_we8", [8, NT], F32)
    probs = sb("a2_probs", [128, 32], F32)
    probs2 = sb("a2_probs2", [128, 32], F32)
    misc = sb("a2_misc", [128, 16], F32)
    gb = sb("a2_gb", [128, NT], F32)
    ub = sb("a2_ub", [128, NT], F32)
    hg = sb("a2_hg", [128, NT], F32)
    hu = sb("a2_hu", [128, NT], F32)
    hbf = sb("a2_hbf", [128, NT], BF16)
    xres = sb("a2_xres", [128, NT], F32)
    sq = [sb("a2_sq0", [128, NT], F32), sb("a2_sq1", [128, NT], F32)]
    stats = sb("a2_stats", [1, 4 * NT], F32)
    ones = sb("a2_ones", [128, 128], F32)
    ident = sb("a2_ident", [128, 128], F32)
    sel = sb("a2_sel", [8, 128], F32)
    outt = sb("a2_outt", [128, 2, NT], F32)
    eps = sb("a2_eps", [1, 1], F32)
    pb = [ps("a2_pb%d" % i) for i in range(8)]

    prog.op("vector", lambda e: e.memset(eps[:, :], EPS))
    prog.dma("sync", lambda e: e.dma_start(out=ones[:, :], in_=ones_d[:, :]))
    prog.dma("sync", lambda e: e.dma_start(out=ident[:, :], in_=ident_d[:, :]))
    prog.dma("sync", lambda e: e.dma_start(out=sel[:, :], in_=sel_d[:, :]))
    prog.dma("sync", lambda e: e.dma_start(
        out=stage[:, 0, :].rearrange("p (a m) -> p a m", m=8), in_=wrr[:, :, :]))
    prog.new_phase()
    prog.op("scalar", lambda e: e.copy(
        out=wrbf[:, :, :].rearrange("p a m -> p (a m)"), in_=stage[:, 0, :]))
    prog.new_phase()

    for g in range(NG):
        gs = slice(g * NT, (g + 1) * NT)
        for kt in range(DKT):
            prog.dma("sync", lambda e, kt=kt, gs=gs: e.dma_start(
                out=sq[0][:, :], in_=x1r[:, kt, gs]))
            prog.new_phase()
            prog.op("scalar", lambda e, kt=kt: e.copy(out=x1bf[:, kt, :], in_=sq[0][:, :]))
            prog.new_phase()
        for sub in range(4):
            for k in range(DKT):
                prog.op("tensor", lambda e, sub=sub, k=k: e.matmul(
                    pb[sub][:, 0:8], x1bf[:, k, sub * 128:(sub + 1) * 128],
                    wrbf[:, k, :], start=(k == 0), stop=(k == DKT - 1)))
        prog.new_phase()
        for sub in range(4):
            prog.op("vector", lambda e, sub=sub: e.reduce_max(
                misc[:, sub:sub + 1], pb[sub][:, 0:8], AX.X))
        prog.new_phase()
        for sub in range(4):
            prog.op("vector", lambda e, sub=sub: e.tensor_scalar(
                probs[:, sub * 8:(sub + 1) * 8], pb[sub][:, 0:8],
                misc[:, sub:sub + 1], None, ALU.subtract))
        prog.new_phase()
        prog.op("scalar", lambda e: e.activation(probs[:, :], probs[:, :], AF.Exp))
        prog.new_phase()
        for sub in range(4):
            prog.op("vector", lambda e, sub=sub: e.reduce_sum(
                misc[:, 4 + sub:5 + sub], probs[:, sub * 8:(sub + 1) * 8], AX.X))
        prog.new_phase()
        prog.op("vector", lambda e: e.reciprocal(misc[:, 4:8], misc[:, 4:8]))
        prog.new_phase()
        for sub in range(4):
            prog.op("vector", lambda e, sub=sub: e.tensor_scalar_mul(
                probs[:, sub * 8:(sub + 1) * 8], probs[:, sub * 8:(sub + 1) * 8],
                misc[:, 4 + sub:5 + sub]))
        prog.new_phase()
        for sub in range(4):
            prog.op("vector", lambda e, sub=sub: e.reduce_max(
                misc[:, sub:sub + 1], probs[:, sub * 8:(sub + 1) * 8], AX.X))
        prog.new_phase()
        for sub in range(4):
            prog.op("vector", lambda e, sub=sub: e.tensor_scalar(
                probs2[:, sub * 8:(sub + 1) * 8], probs[:, sub * 8:(sub + 1) * 8],
                misc[:, sub:sub + 1], -1.0e9, ALU.is_ge, ALU.mult))
        prog.new_phase()
        prog.op("vector", lambda e: e.tensor_add(probs2[:, :], probs2[:, :], probs[:, :]))
        prog.new_phase()
        for sub in range(4):
            prog.op("vector", lambda e, sub=sub: e.reduce_max(
                misc[:, 8 + sub:9 + sub], probs2[:, sub * 8:(sub + 1) * 8], AX.X))
        prog.new_phase()
        for sub in range(4):
            prog.op("vector", lambda e, sub=sub: e.tensor_scalar(
                probs2[:, sub * 8:(sub + 1) * 8], probs[:, sub * 8:(sub + 1) * 8],
                misc[:, 8 + sub:9 + sub], None, ALU.is_ge))
        prog.new_phase()
        prog.op("vector", lambda e: e.tensor_mul(probs[:, :], probs[:, :], probs2[:, :]))
        prog.new_phase()
        for sub in range(4):
            prog.op("tensor", lambda e, sub=sub: e.matmul(
                pb[4][0:8, sub * 128:(sub + 1) * 128], probs[:, sub * 8:(sub + 1) * 8],
                ident[:, :], start=(sub == 0), stop=(sub == 3), skip_group_check=True))
        prog.new_phase()
        prog.op("vector", lambda e: e.tensor_copy(out=we8[:, :], in_=pb[4][0:8, :]))
        prog.new_phase()
        for e_ in range(NE):
            for w_ in range(2):
                src = agr if w_ == 0 else aur
                dst = tg_bf if w_ == 0 else tu_bf
                prog.dma("sync", lambda e, e_=e_, src=src: e.dma_start(
                    out=stage[:, 0:2, :].rearrange("p a b -> p (a b)")
                        .rearrange("p (a m) -> p a m", m=16),
                    in_=src[:, e_ * 16:(e_ + 1) * 16, :]))
                prog.new_phase()
                prog.op("scalar", lambda e: e.copy(
                    out=agbf[:, :, :].rearrange("p a m -> p (a m)"),
                    in_=stage[:, 0:2, :].rearrange("p a b -> p (a b)")))
                prog.new_phase()
                for k in range(DKT):
                    prog.op("tensor", lambda e, k=k: e.matmul(
                        pb[5][0:16, :], agbf[:, k, :], x1bf[:, k, :],
                        start=(k == 0), stop=(k == DKT - 1)))
                prog.new_phase()
                prog.op("scalar", lambda e, e_=e_, dst=dst: e.copy(
                    out=dst[:, e_, :], in_=pb[5][0:16, :]))
                prog.new_phase()
        prog.op("vector", lambda e: e.memset(s_acc[:, :, :], 0.0))
        prog.new_phase()
        for ft in range(NFF0):
            fs = slice(ft * 128, (ft + 1) * 128)
            prog.dma("sync", lambda e, ft=ft, gs=gs: e.dma_start(
                out=gb[:, :], in_=gbr[:, ft, gs]))
            prog.dma("sync", lambda e, ft=ft, gs=gs: e.dma_start(
                out=ub[:, :], in_=ubr[:, ft, gs]))
            prog.dma("sync", lambda e, fs=fs: e.dma_start(out=bst[:, :, :], in_=bgr[:, :, fs]))
            prog.dma("sync", lambda e, ft=ft: e.dma_start(out=adst[:, :, :],
                                                          in_=adr[:, :, ft, :]))
            prog.new_phase()
            prog.op("vector", lambda e: e.tensor_copy(
                out=bgbf[:, :, :].rearrange("p e f -> p (e f)"),
                in_=bst[:, :, :].rearrange("p e f -> p (e f)")))
            prog.op("scalar", lambda e: e.copy(
                out=adbf[:, :, :].rearrange("p e m -> p (e m)"),
                in_=adst[:, :, :].rearrange("p e m -> p (e m)")))
            prog.new_phase()
            prog.dma("sync", lambda e, fs=fs: e.dma_start(out=bst[:, :, :], in_=bur[:, :, fs]))
            prog.new_phase()
            prog.op("gpsimd", lambda e: e.tensor_copy(
                out=bubf[:, :, :].rearrange("p e f -> p (e f)"),
                in_=bst[:, :, :].rearrange("p e f -> p (e f)")))
            prog.new_phase()
            for e_ in range(NE):
                prog.op("tensor", lambda e, e_=e_: e.matmul(
                    pb[2][:, :], bgbf[:, e_, :], tg_bf[:, e_, :], start=True, stop=True))
                prog.op("tensor", lambda e, e_=e_: e.matmul(
                    pb[3][:, :], bubf[:, e_, :], tu_bf[:, e_, :], start=True, stop=True))
                prog.new_phase()
                prog.op("vector", lambda e: e.scalar_tensor_tensor(
                    hg[:, :], pb[2][:, :], 2.0, gb[:, :], ALU.mult, ALU.add))
                prog.new_phase()
                prog.op("vector", lambda e: e.scalar_tensor_tensor(
                    hu[:, :], pb[3][:, :], 2.0, ub[:, :], ALU.mult, ALU.add))
                prog.op("scalar", lambda e: e.activation(hg[:, :], hg[:, :], AF.Silu))
                prog.new_phase()
                prog.op("vector", lambda e: e.tensor_mul(hbf[:, :], hg[:, :], hu[:, :]))
                prog.new_phase()
                prog.op("tensor", lambda e, e_=e_: e.matmul(
                    pb[6][0:16, :], adbf[:, e_, :], hbf[:, :], start=True, stop=True))
                prog.new_phase()
                prog.op("vector", lambda e, e_=e_: e.tensor_add(
                    s_acc[:, e_, :], s_acc[:, e_, :], pb[6][0:16, :]))
                prog.new_phase()
        for e_ in range(NE):
            prog.op("tensor", lambda e, e_=e_: e.matmul(
                pb[7][0:16, :], sel[:, e_ * 16:(e_ + 1) * 16], we8[:, :],
                start=True, stop=True))
            prog.new_phase()
            prog.op("vector", lambda e, e_=e_: e.tensor_mul(
                s_bf[:, e_, :], s_acc[:, e_, :], pb[7][0:16, :]))
            prog.new_phase()
        for dt in range(DKT):
            ds = slice(dt * 128, (dt + 1) * 128)
            prog.dma("sync", lambda e, ds=ds: e.dma_start(out=bst[:, :, :], in_=bdr[:, :, ds]))
            prog.dma("sync", lambda e, dt=dt, gs=gs: e.dma_start(
                out=xres[:, :], in_=ybr[:, dt, gs]))
            prog.new_phase()
            prog.op("scalar", lambda e: e.copy(
                out=bdbf[:, :, :].rearrange("p e f -> p (e f)"),
                in_=bst[:, :, :].rearrange("p e f -> p (e f)")))
            prog.new_phase()
            for e_ in range(NE):
                prog.op("tensor", lambda e, e_=e_: e.matmul(
                    pb[7][:, :], bdbf[:, e_, :], s_bf[:, e_, :],
                    start=(e_ == 0), stop=(e_ == NE - 1)))
            prog.new_phase()
            prog.op("vector", lambda e, dt=dt: e.tensor_add(
                ybuf[:, dt, :], pb[7][:, :], xres[:, :]))
            prog.new_phase()
        lnc = {"ones": ones, "pb_s1": pb[0], "pb_s2": pb[1], "pb_A": pb[2],
               "pb_B": pb[3], "stats": stats, "sq": sq, "eps": eps[0:1, 0:1]}

        def out_write(kt, g=g):
            dst = outt[:, kt % 2, :]

            def post(p, kt=kt, g=g):
                p.dma("gpsimd", lambda e, kt=kt, g=g: e.dma_start(
                    out=x2r[:, kt, g * NT:(g + 1) * NT], in_=outt[:, kt % 2, :]))
                if bnd is not None and g == 0:
                    p.dma("gpsimd", lambda e, kt=kt: e.dma_start(
                        out=bnd[0, kt * 128:(kt + 1) * 128].rearrange(
                            "(p one) -> p one", one=1),
                        in_=outt[:, kt % 2, 0:1]))
                if bnd is not None and g == NG - 1:
                    p.dma("gpsimd", lambda e, kt=kt: e.dma_start(
                        out=bnd[1, kt * 128:(kt + 1) * 128].rearrange(
                            "(p one) -> p one", one=1),
                        in_=outt[:, kt % 2, NT - 1:NT]))
            return dst, post
        ln_group(prog, lnc, lambda kt: ybuf[:, kt, :], out_write)


def sec_E(prog, sb, ps, io):
    (x2r, wur, wvr, wg1r, wg2r, bu_pt, bv_pt, bg1_pt, bg2_pt, cw_pt, cb_pt,
     tor, ugr, vgr) = (io["x2r"], io["wur"], io["wvr"], io["wg1r"], io["wg2r"],
                       io["bu_pt"], io["bv_pt"], io["bg1_pt"], io["bg2_pt"],
                       io["cw_pt"], io["cb_pt"], io["tor"], io["ugr"], io["vgr"])
    stage1 = sb("aE_stage1", [128, TOK], F32)
    x2bf = sb("aE_x2bf", [128, DKT, TOK], BF16)
    tconv = sb("aE_tconv", [128, TOK], F32)
    cw = sb("aE_cw", [128, DKT, 3], F32)
    cb = sb("aE_cb", [128, DKT], F32)
    bu = sb("aE_bu", [128, DKT], F32)
    bv = sb("aE_bv", [128, DKT], F32)
    bg1 = sb("aE_bg1", [128, 4], F32)
    bg2 = sb("aE_bg2", [128, DKT], F32)
    wsl = sb("aE_wsl", [128, 8, 128], F32)
    wbf = sb("aE_wbf", [128, DKT, 128], BF16)
    hbf = sb("aE_hbf", [128, 4, TOK], BF16)
    gT = sb("aE_gT", [128, DKT, TOK], F32)
    oscr = sb("aE_oscr", [128, TOK], F32)
    hl = sb("aE_hl", [128, DKT], F32)
    hr = sb("aE_hr", [128, DKT], F32)
    pb = [ps("aE_pb%d" % i) for i in range(2)]

    prog.dma("sync", lambda e: e.dma_start(out=cw[:, :, :], in_=cw_pt[:, :, :]))
    prog.dma("sync", lambda e: e.dma_start(out=cb[:, :], in_=cb_pt[:, :]))
    prog.dma("sync", lambda e: e.dma_start(out=bu[:, :], in_=bu_pt[:, :]))
    prog.dma("sync", lambda e: e.dma_start(out=bv[:, :], in_=bv_pt[:, :]))
    prog.dma("sync", lambda e: e.dma_start(out=bg1[:, :], in_=bg1_pt[:, :]))
    prog.dma("sync", lambda e: e.dma_start(out=bg2[:, :], in_=bg2_pt[:, :]))
    prog.new_phase()
    if "halo_l_pt" in io:
        prog.dma("sync", lambda e: e.dma_start(out=hl[:, :], in_=io["halo_l_pt"]))
        prog.dma("sync", lambda e: e.dma_start(out=hr[:, :], in_=io["halo_r_pt"]))
        prog.new_phase()
    else:
        bgath = sb("aE_bgath", [16, D], F32)
        oh = sb("aE_oh", [16, 2], F32)
        prog.dma("sync", lambda e: e.dma_start(out=bgath[:, :], in_=io["bnd_g"]))
        prog.dma("sync", lambda e: e.dma_start(out=oh[:, :], in_=io["oh"]))
        prog.new_phase()
        for kt in range(DKT):
            prog.op("tensor", lambda e, kt=kt: e.matmul(
                pb[0][:, kt:kt + 1], bgath[:, kt * 128:(kt + 1) * 128], oh[:, 0:1],
                start=(kt == 0), stop=(kt == DKT - 1), skip_group_check=True))
            prog.op("tensor", lambda e, kt=kt: e.matmul(
                pb[1][:, kt:kt + 1], bgath[:, kt * 128:(kt + 1) * 128], oh[:, 1:2],
                start=(kt == 0), stop=(kt == DKT - 1), skip_group_check=True))
        prog.new_phase()
        prog.op("vector", lambda e: e.tensor_copy(out=hl[:, :], in_=pb[0][:, 0:DKT]))
        prog.op("scalar", lambda e: e.copy(out=hr[:, :], in_=pb[1][:, 0:DKT]))
        prog.new_phase()
    for kt in range(DKT):
        prog.dma("sync", lambda e, kt=kt: e.dma_start(out=stage1[:, :], in_=x2r[:, kt, :]))
        prog.new_phase()
        prog.op("scalar", lambda e, kt=kt: e.copy(out=x2bf[:, kt, :], in_=stage1[:, :]))
        prog.op("vector", lambda e, kt=kt: e.tensor_scalar_mul(
            tconv[:, :], stage1[:, :], cw[:, kt, 1:2]))
        prog.new_phase()
        prog.op("vector", lambda e, kt=kt: e.scalar_tensor_tensor(
            tconv[:, 1:], stage1[:, :TOK - 1], cw[:, kt, 0:1], tconv[:, 1:],
            ALU.mult, ALU.add))
        prog.new_phase()
        prog.op("vector", lambda e, kt=kt: e.scalar_tensor_tensor(
            tconv[:, :TOK - 1], stage1[:, 1:], cw[:, kt, 2:3], tconv[:, :TOK - 1],
            ALU.mult, ALU.add))
        prog.new_phase()
        prog.op("vector", lambda e, kt=kt: e.tensor_scalar_add(
            tconv[:, :], tconv[:, :], cb[:, kt:kt + 1]))
        prog.new_phase()
        prog.op("vector", lambda e, kt=kt: e.scalar_tensor_tensor(
            tconv[:, 0:1], hl[:, kt:kt + 1], cw[:, kt, 0:1], tconv[:, 0:1],
            ALU.mult, ALU.add))
        prog.op("vector", lambda e, kt=kt: e.scalar_tensor_tensor(
            tconv[:, TOK - 1:TOK], hr[:, kt:kt + 1], cw[:, kt, 2:3],
            tconv[:, TOK - 1:TOK], ALU.mult, ALU.add))
        prog.new_phase()
        prog.dma("gpsimd", lambda e, kt=kt: e.dma_start(out=tor[:, kt, :], in_=tconv[:, :]))
        prog.new_phase()
    for ht in range(4):
        for hh in range(2):
            prog.dma("sync", lambda e, ht=ht, hh=hh: e.dma_start(
                out=wsl[:, :, :],
                in_=wg1r[:, hh * 8:(hh + 1) * 8, ht * 128:(ht + 1) * 128]))
            prog.new_phase()
            prog.op("scalar", lambda e, hh=hh: e.copy(
                out=wbf[:, hh * 8:(hh + 1) * 8, :].rearrange("p a b -> p (a b)"),
                in_=wsl[:, :, :].rearrange("p a b -> p (a b)")))
            prog.new_phase()
        for g in range(2):
            for k in range(DKT):
                prog.op("tensor", lambda e, g=g, k=k: e.matmul(
                    pb[g][:, :], wbf[:, k, :],
                    x2bf[:, k, g * 512:(g + 1) * 512],
                    start=(k == 0), stop=(k == DKT - 1)))
        prog.new_phase()
        for g in range(2):
            prog.op("scalar", lambda e, ht=ht, g=g: e.activation(
                hbf[:, ht, g * 512:(g + 1) * 512], pb[g][:, :], AF.Relu,
                bias=bg1[:, ht:ht + 1]))
        prog.new_phase()
    for ct in range(DKT):
        prog.dma("sync", lambda e, ct=ct: e.dma_start(
            out=wsl[:, 0:4, :], in_=wg2r[:, :, ct * 128:(ct + 1) * 128]))
        prog.new_phase()
        prog.op("scalar", lambda e: e.copy(
            out=wbf[:, 0:4, :].rearrange("p a b -> p (a b)"),
            in_=wsl[:, 0:4, :].rearrange("p a b -> p (a b)")))
        prog.new_phase()
        for g in range(2):
            for k in range(4):
                prog.op("tensor", lambda e, g=g, k=k: e.matmul(
                    pb[g][:, :], wbf[:, k, :], hbf[:, k, g * 512:(g + 1) * 512],
                    start=(k == 0), stop=(k == 3)))
        prog.new_phase()
        for g in range(2):
            prog.op("scalar", lambda e, ct=ct, g=g: e.activation(
                gT[:, ct, g * 512:(g + 1) * 512], pb[g][:, :], AF.Sigmoid,
                bias=bg2[:, ct:ct + 1]))
        prog.new_phase()
    for (wr_ap, b_sb, out_r) in ((wur, bu, ugr), (wvr, bv, vgr)):
        for ct in range(DKT):
            for hh in range(2):
                prog.dma("sync", lambda e, ct=ct, hh=hh, wr_ap=wr_ap: e.dma_start(
                    out=wsl[:, :, :],
                    in_=wr_ap[:, hh * 8:(hh + 1) * 8, ct * 128:(ct + 1) * 128]))
                prog.new_phase()
                prog.op("scalar", lambda e, hh=hh: e.copy(
                    out=wbf[:, hh * 8:(hh + 1) * 8, :].rearrange("p a b -> p (a b)"),
                    in_=wsl[:, :, :].rearrange("p a b -> p (a b)")))
                prog.new_phase()
            for g in range(2):
                for k in range(DKT):
                    prog.op("tensor", lambda e, g=g, k=k: e.matmul(
                        pb[g][:, :], wbf[:, k, :],
                        x2bf[:, k, g * 512:(g + 1) * 512],
                        start=(k == 0), stop=(k == DKT - 1)))
            prog.new_phase()
            for g in range(2):
                prog.op("vector", lambda e, ct=ct, g=g, b_sb=b_sb: e.scalar_tensor_tensor(
                    oscr[:, g * 512:(g + 1) * 512], pb[g][:, :], b_sb[:, ct:ct + 1],
                    gT[:, ct, g * 512:(g + 1) * 512], ALU.add, ALU.mult))
            prog.new_phase()
            prog.dma("gpsimd", lambda e, ct=ct, out_r=out_r: e.dma_start(
                out=out_r[:, ct, :], in_=oscr[:, :]))
            prog.new_phase()


def sec_F(prog, sb, ps, io):
    ug3 = io["ug3"]
    vg3 = io["vg3"]
    go3 = io["go3"]
    u_sb = sb("aF_u", [128, T], F32)
    v_sb = sb("aF_v", [128, T], F32)
    cs = sb("aF_cs", [128, T], F32)
    v3 = lambda t: t[:, :].rearrange("p (r t2) -> p r t2", r=8)
    for h in range(2):
        prog.dma("sync", lambda e, h=h: e.dma_start(out=v3(u_sb), in_=ug3(h)))
        prog.dma("sync", lambda e, h=h: e.dma_start(out=v3(v_sb), in_=vg3(h)))
        prog.new_phase()
        prog.op("vector", lambda e: e.tensor_tensor_scan(
            cs[:, :], u_sb[:, :], u_sb[:, :], 0.0, ALU.add, ALU.bypass))
        prog.new_phase()
        prog.op("vector", lambda e: e.tensor_mul(cs[:, :], cs[:, :], v_sb[:, :]))
        prog.new_phase()
        prog.dma("gpsimd", lambda e, h=h: e.dma_start(out=go3(h), in_=v3(cs)))
        prog.new_phase()


def sec_G(prog, sb, ps, io):
    x2r, tor, gor, wur, ones_d, x3r = (io["x2r"], io["tor"], io["gor"],
                                       io["wur"], io["ones_d"], io["x3r"])
    x2sl = sb("aG_x2sl", [128, DKT, TOK], F32)
    tsl = sb("aG_tsl", [128, DKT, TOK], F32)
    gobf = sb("aG_gobf", [128, DKT, TOK], BF16)
    stage = sb("aG_stage", [128, TOK], F32)
    wsl = sb("aG_wsl", [128, 8, 128], F32)
    wbf = sb("aG_wbf", [128, DKT, 128], BF16)
    scr = sb("aG_scr", [128, 2, 512], F32)
    sq = [sb("aG_sq0", [128, NT], F32), sb("aG_sq1", [128, NT], F32)]
    stats = sb("aG_stats", [1, 4 * NT], F32)
    ones = sb("aG_ones", [128, 128], F32)
    eps = sb("aG_eps", [1, 1], F32)
    pb = [ps("aG_pb%d" % i) for i in range(6)]

    prog.op("vector", lambda e: e.memset(eps[:, :], EPS))
    for ch in range(4):
        prog.dma("sync", lambda e, ch=ch: e.dma_start(
            out=x2sl[:, ch * 4:(ch + 1) * 4, :], in_=x2r[:, ch * 4:(ch + 1) * 4, :]))
        prog.dma("sync", lambda e, ch=ch: e.dma_start(
            out=tsl[:, ch * 4:(ch + 1) * 4, :], in_=tor[:, ch * 4:(ch + 1) * 4, :]))
    prog.dma("sync", lambda e: e.dma_start(out=ones[:, :], in_=ones_d[:, :]))
    prog.new_phase()
    for kt in range(DKT):
        prog.dma("sync", lambda e, kt=kt: e.dma_start(out=stage[:, :], in_=gor[:, kt, :]))
        prog.new_phase()
        prog.op("scalar", lambda e, kt=kt: e.copy(out=gobf[:, kt, :], in_=stage[:, :]))
        prog.new_phase()
    for ct in range(DKT):
        for hh in range(2):
            prog.dma("sync", lambda e, ct=ct, hh=hh: e.dma_start(
                out=wsl[:, :, :],
                in_=wur[:, hh * 8:(hh + 1) * 8, ct * 128:(ct + 1) * 128]))
            prog.new_phase()
            prog.op("scalar", lambda e, hh=hh: e.copy(
                out=wbf[:, hh * 8:(hh + 1) * 8, :].rearrange("p a b -> p (a b)"),
                in_=wsl[:, :, :].rearrange("p a b -> p (a b)")))
            prog.new_phase()
        for g in range(2):
            for k in range(DKT):
                prog.op("tensor", lambda e, g=g, k=k: e.matmul(
                    pb[g][:, :], wbf[:, k, :], gobf[:, k, g * 512:(g + 1) * 512],
                    start=(k == 0), stop=(k == DKT - 1)))
        prog.new_phase()
        for g in range(2):
            prog.op("vector", lambda e, ct=ct, g=g: e.tensor_add(
                scr[:, g, :], pb[g][:, :], tsl[:, ct, g * 512:(g + 1) * 512]))
        prog.new_phase()
        for g in range(2):
            prog.op("gpsimd", lambda e, ct=ct, g=g: e.tensor_add(
                x2sl[:, ct, g * 512:(g + 1) * 512],
                x2sl[:, ct, g * 512:(g + 1) * 512], scr[:, g, :]))
        prog.new_phase()
    lnc = {"ones": ones, "pb_s1": pb[2], "pb_s2": pb[3], "pb_A": pb[4], "pb_B": pb[5],
           "stats": stats, "sq": sq, "eps": eps[0:1, 0:1]}
    for g in range(NG):
        def out_write(kt, g=g):
            return tsl[:, kt, g * NT:(g + 1) * NT], None
        ln_group(prog, lnc, lambda kt, g=g: x2sl[:, kt, g * NT:(g + 1) * NT], out_write)
    for ch in range(4):
        prog.dma("gpsimd", lambda e, ch=ch: e.dma_start(
            out=x3r[:, ch * 4:(ch + 1) * 4, :], in_=tsl[:, ch * 4:(ch + 1) * 4, :]))
    prog.new_phase()


def sec_H(prog, sb, ps, io):
    (x3r, su_ug, su_dp, wsc_up, wsc_gt, wsc_dp, wupr, wgtr, wdnr, h128_d,
     ones_d, x4r) = (io["x3r"], io["su_ug"], io["su_dp"], io["wsc_up"],
                     io["wsc_gt"], io["wsc_dp"], io["wupr"], io["wgtr"],
                     io["wdnr"], io["h128_d"], io["ones_d"], io["x4r"])
    C1 = 1.0 / (np.sqrt(2048.0) * 64.0)
    C2 = 1.0 / (np.sqrt(8192.0) * 64.0)
    zmid = sb("aH_zmid", [128, NFF1, NT], BF16)
    z1f = sb("aH_z1f", [128, DKT, NT], F32)
    z1bf = sb("aH_z1bf", [128, DKT, NT], BF16)
    tmpb = sb("aH_tmpb", [128, 8, NT], F32)
    stage = sb("aH_stage", [128, DKT, 128], F32)
    wbf_a = sb("aH_wbfa", [128, DKT, 128], BF16)
    wdnbf = sb("aH_wdnbf", [128, NFF1, 128], BF16)
    xs = sb("aH_xs", [128, NT], F32)
    xsbf = sb("aH_xsbf", [128, NT], BF16)
    upsb = sb("aH_upsb", [128, NT], F32)
    sgsb = sb("aH_sgsb", [128, NT], F32)
    xres = sb("aH_xres", [128, NT], F32)
    su1 = sb("aH_su1", [128, DKT], F32)
    su2 = sb("aH_su2", [128, NFF1], F32)
    w64u = sb("aH_w64u", [128, NFF1], F32)
    w64g = sb("aH_w64g", [128, NFF1], F32)
    w64d = sb("aH_w64d", [128, DKT], F32)
    h128 = sb("aH_h128", [128, 128], F32)
    h128bf = sb("aH_h128bf", [128, 128], BF16)
    sq = [sb("aH_sq0", [128, NT], F32), sb("aH_sq1", [128, NT], F32)]
    stats = sb("aH_stats", [1, 4 * NT], F32)
    ones = sb("aH_ones", [128, 128], F32)
    outt = sb("aH_outt", [128, 2, NT], F32)
    eps = sb("aH_eps", [1, 1], F32)
    pb = [ps("aH_pb%d" % i) for i in range(6)]

    prog.op("vector", lambda e: e.memset(eps[:, :], EPS))
    prog.dma("sync", lambda e: e.dma_start(out=su1[:, :], in_=su_ug))
    prog.dma("sync", lambda e: e.dma_start(out=su2[:, :], in_=su_dp))
    prog.dma("sync", lambda e: e.dma_start(out=w64u[:, :], in_=wsc_up))
    prog.dma("sync", lambda e: e.dma_start(out=w64g[:, :], in_=wsc_gt))
    prog.dma("sync", lambda e: e.dma_start(out=w64d[:, :], in_=wsc_dp))
    prog.dma("sync", lambda e: e.dma_start(out=h128[:, :], in_=h128_d))
    prog.dma("sync", lambda e: e.dma_start(out=ones[:, :], in_=ones_d))
    prog.new_phase()
    prog.op("scalar", lambda e: e.mul(w64u[:, :], w64u[:, :], 64.0))
    prog.op("vector", lambda e: e.tensor_scalar_mul(w64g[:, :], w64g[:, :], 64.0))
    prog.op("gpsimd", lambda e: e.tensor_scalar_mul(w64d[:, :], w64d[:, :], 64.0))
    prog.new_phase()
    prog.op("scalar", lambda e: e.copy(out=h128bf[:, :], in_=h128[:, :]))
    prog.new_phase()

    for g in range(NG):
        gs = slice(g * NT, (g + 1) * NT)
        for kt in range(DKT):
            prog.dma("sync", lambda e, kt=kt, gs=gs: e.dma_start(
                out=xs[:, :], in_=x3r[:, kt, gs]))
            prog.new_phase()
            prog.op("vector", lambda e, kt=kt: e.tensor_scalar_mul(
                xs[:, :], xs[:, :], su1[:, kt:kt + 1]))
            prog.new_phase()
            prog.op("scalar", lambda e: e.copy(out=xsbf[:, :], in_=xs[:, :]))
            prog.new_phase()
            prog.op("tensor", lambda e: e.matmul(
                pb[0][:, :], h128bf[:, :], xsbf[:, :], start=True, stop=True))
            prog.new_phase()
            prog.op("scalar", lambda e, kt=kt: e.activation(
                z1f[:, kt, :], pb[0][:, :], AF.Identity, scale=C1))
            prog.new_phase()
        s = 1
        while s < DKT:
            pairs = [(i, i + s) for i in range(DKT) if (i // s) % 2 == 0]
            for ci in range(0, len(pairs), 8):
                chunk = pairs[ci:ci + 8]
                for pi, (a, b) in enumerate(chunk):
                    prog.op("vector", lambda e, a=a, b=b, pi=pi: e.tensor_sub(
                        tmpb[:, pi, :], z1f[:, a, :], z1f[:, b, :]))
                prog.new_phase()
                for pi, (a, b) in enumerate(chunk):
                    prog.op("vector", lambda e, a=a, b=b: e.tensor_add(
                        z1f[:, a, :], z1f[:, a, :], z1f[:, b, :]))
                prog.new_phase()
                for pi, (a, b) in enumerate(chunk):
                    prog.op("gpsimd", lambda e, b=b, pi=pi: e.tensor_copy(
                        out=z1f[:, b, :], in_=tmpb[:, pi, :]))
                prog.new_phase()
            s *= 2
        prog.op("scalar", lambda e: e.copy(
            out=z1bf[:, :, :].rearrange("p a t -> p (a t)"),
            in_=z1f[:, :, :].rearrange("p a t -> p (a t)")))
        prog.new_phase()
        for ft in range(NFF1):
            fs = slice(ft * 128, (ft + 1) * 128)
            prog.dma("sync", lambda e, fs=fs: e.dma_start(
                out=stage[:, :, :], in_=wupr[:, :, fs]))
            prog.new_phase()
            prog.op("scalar", lambda e: e.copy(
                out=wbf_a[:, :, :].rearrange("p a b -> p (a b)"),
                in_=stage[:, :, :].rearrange("p a b -> p (a b)")))
            prog.new_phase()
            prog.dma("sync", lambda e, fs=fs: e.dma_start(
                out=stage[:, :, :], in_=wgtr[:, :, fs]))
            for k in range(DKT):
                prog.op("tensor", lambda e, k=k: e.matmul(
                    pb[0][:, :], wbf_a[:, k, :], z1bf[:, k, :],
                    start=(k == 0), stop=(k == DKT - 1)))
            prog.new_phase()
            prog.op("vector", lambda e, ft=ft: e.tensor_scalar_mul(
                upsb[:, :], pb[0][:, :], w64u[:, ft:ft + 1]))
            prog.op("scalar", lambda e: e.copy(
                out=wbf_a[:, :, :].rearrange("p a b -> p (a b)"),
                in_=stage[:, :, :].rearrange("p a b -> p (a b)")))
            prog.new_phase()
            for k in range(DKT):
                prog.op("tensor", lambda e, k=k: e.matmul(
                    pb[1][:, :], wbf_a[:, k, :], z1bf[:, k, :],
                    start=(k == 0), stop=(k == DKT - 1)))
            prog.new_phase()
            prog.op("scalar", lambda e, ft=ft: e.activation(
                sgsb[:, :], pb[1][:, :], AF.Silu, scale=w64g[:, ft:ft + 1]))
            prog.new_phase()
            prog.op("vector", lambda e, ft=ft: e.tensor_mul(
                zmid[:, ft, :], sgsb[:, :], upsb[:, :]))
            prog.new_phase()
        for ft in range(NFF1):
            prog.op("vector", lambda e, ft=ft: e.tensor_scalar_mul(
                zmid[:, ft, :], zmid[:, ft, :], su2[:, ft:ft + 1]))
        prog.new_phase()
        for ft in range(NFF1 + 1):
            if ft < NFF1:
                prog.op("tensor", lambda e, ft=ft: e.matmul(
                    pb[0 + ft % 2][:, :], h128bf[:, :], zmid[:, ft, :],
                    start=True, stop=True))
            if ft > 0:
                prog.op("scalar", lambda e, ft=ft: e.copy(
                    out=zmid[:, ft - 1, :], in_=pb[0 + (ft - 1) % 2][:, :]))
            prog.new_phase()
        s = 1
        while s < NFF1:
            pairs = [(i, i + s) for i in range(NFF1) if (i // s) % 2 == 0]
            for ci in range(0, len(pairs), 8):
                chunk = pairs[ci:ci + 8]
                for pi, (a, b) in enumerate(chunk):
                    prog.op("vector", lambda e, a=a, b=b, pi=pi: e.tensor_sub(
                        tmpb[:, pi, :], zmid[:, a, :], zmid[:, b, :]))
                prog.new_phase()
                for pi, (a, b) in enumerate(chunk):
                    prog.op("vector", lambda e, a=a, b=b: e.tensor_add(
                        zmid[:, a, :], zmid[:, a, :], zmid[:, b, :]))
                prog.new_phase()
                for pi, (a, b) in enumerate(chunk):
                    prog.op("gpsimd", lambda e, b=b, pi=pi: e.tensor_copy(
                        out=zmid[:, b, :], in_=tmpb[:, pi, :]))
                prog.new_phase()
            s *= 2
        for dt in range(DKT):
            ds = slice(dt * 128, (dt + 1) * 128)
            for ch in range(4):
                ftsl = slice(ch * 16, (ch + 1) * 16)
                prog.dma("sync", lambda e, ds=ds, ftsl=ftsl: e.dma_start(
                    out=stage[:, :, :], in_=wdnr[:, ftsl, ds]))
                prog.new_phase()
                prog.op("scalar", lambda e, ftsl=ftsl: e.copy(
                    out=wdnbf[:, ftsl, :].rearrange("p a b -> p (a b)"),
                    in_=stage[:, :, :].rearrange("p a b -> p (a b)")))
                prog.new_phase()
            prog.dma("sync", lambda e, dt=dt, gs=gs: e.dma_start(
                out=xres[:, :], in_=x3r[:, dt, gs]))
            prog.new_phase()
            for ft in range(NFF1):
                prog.op("tensor", lambda e, ft=ft: e.matmul(
                    pb[2][:, :], wdnbf[:, ft, :], zmid[:, ft, :],
                    start=(ft == 0), stop=(ft == NFF1 - 1)))
            prog.new_phase()
            prog.op("vector", lambda e, dt=dt: e.tensor_scalar(
                sgsb[:, :], pb[2][:, :], w64d[:, dt:dt + 1], C2, ALU.mult, ALU.mult))
            prog.new_phase()
            prog.op("gpsimd", lambda e, dt=dt: e.tensor_add(
                z1f[:, dt, :], sgsb[:, :], xres[:, :]))
            prog.new_phase()
        lnc = {"ones": ones, "pb_s1": pb[0], "pb_s2": pb[1], "pb_A": pb[3],
               "pb_B": pb[4], "stats": stats, "sq": sq, "eps": eps[0:1, 0:1]}

        def out_write(kt, g=g):
            dst = outt[:, kt % 2, :]

            def post(p, kt=kt, g=g):
                p.dma("gpsimd", lambda e, kt=kt, g=g: e.dma_start(
                    out=x4r[:, kt, g * NT:(g + 1) * NT], in_=outt[:, kt % 2, :]))
            return dst, post
        ln_group(prog, lnc, lambda kt: z1f[:, kt, :], out_write)


def build_FUSED():
    nc = bass.Bass("TRN2", target_bir_lowering=False)
    ein = lambda n, s: nc.dram_tensor(n, s, F32, kind="ExternalInput").ap()
    intr = lambda n, s: nc.dram_tensor(n, s, F32).ap()

    x_ext = ein("x_ext", [D, TOK + 1])
    wactT = ein("wactT", [2 * D, D])
    wrecT = ein("wrecT", [2 * D, D])
    wvalT = ein("wvalT", [D, D])
    wgT = ein("wgT", [D, FF0])
    wuT0 = ein("wuT0", [D, FF0])
    wdT = ein("wdT", [FF0, D])
    wrT = ein("wrT", [D, 8])
    agT = ein("agT", [NE * D, 16])
    auT = ein("auT", [NE * D, 16])
    bgT = ein("bgT", [NE * 16, FF0])
    buT = ein("buT", [NE * 16, FF0])
    adT = ein("adT", [NE * FF0, 16])
    bdT = ein("bdT", [NE * 16, D])
    sel_d = ein("sel", [8, 128])
    ident_d = ein("ident", [128, 128])
    ones_d = ein("ones", [128, 128])
    wuT1 = ein("wuT1", [D, D])
    wvT1 = ein("wvT1", [D, D])
    wg1T = ein("wg1T", [D, 512])
    wg2T = ein("wg2T", [512, D])
    bu_pt = ein("bu_pt", [128, DKT])
    bv_pt = ein("bv_pt", [128, DKT])
    bg1_pt = ein("bg1_pt", [128, 4])
    bg2_pt = ein("bg2_pt", [128, DKT])
    cw_pt = ein("cw_pt", [128, DKT, 3])
    cb_pt = ein("cb_pt", [128, DKT])
    oh_d = ein("oh", [16, 2])
    su_ug = ein("su_ug", [128, DKT])
    su_dp = ein("su_dp", [128, NFF1])
    wsc_up = ein("wsc_up", [128, NFF1])
    wsc_gt = ein("wsc_gt", [128, NFF1])
    wsc_dp = ein("wsc_dp", [128, DKT])
    wupT = ein("wupT", [D, FF1])
    wgtT = ein("wgtT", [D, FF1])
    wdnT = ein("wdnT", [FF1, D])
    h128_d = ein("h128", [128, 128])
    x4T = nc.dram_tensor("x4T", [D, TOK], F32, kind="ExternalOutput").ap()

    xvT = intr("i_xvT", [D, TOK])
    r4T = intr("i_r4T", [D, TOK])
    xv_rx = intr("i_xvrx", [D, TOK])
    kv_tx = intr("i_kvtx", [D, TOK])
    kv_rx = intr("i_kvrx", [D, TOK])
    x1T = intr("i_x1T", [D, TOK])
    gbT = intr("i_gbT", [FF0, TOK])
    ubT = intr("i_ubT", [FF0, TOK])
    ybT = intr("i_ybT", [D, TOK])
    x2T = intr("i_x2T", [D, TOK])
    bnd = intr("i_bnd", [2, D])
    bnd_g = intr("i_bndg", [16, D])
    toT = intr("i_toT", [D, TOK])
    ugT = intr("i_ugT", [D, TOK])
    vgT = intr("i_vgT", [D, TOK])
    ug_rx = intr("i_ugrx", [D, TOK])
    vg_rx = intr("i_vgrx", [D, TOK])
    go_tx = intr("i_gotx", [D, TOK])
    go_rx = intr("i_gorx", [D, TOK])
    x3T = intr("i_x3T", [D, TOK])

    r16 = lambda ap: ap.rearrange("(kt p) t -> p kt t", p=128)
    prs = []

    with ExitStack() as st_outer:
        F = Fused(nc, st_outer)

        def section(fn, io):
            prog = Prog()
            with ExitStack() as st:
                sb = lambda n, s, d: st.enter_context(nc.sbuf_tensor(n, s, d))
                ps = lambda n: st.enter_context(nc.psum_tensor(n, [128, 512], F32))
                fn(prog, sb, ps, io)
                F.emit_block(prog)

        def cc(kind, in_ap, out_ap):
            prog = Prog()
            prog.cc(lambda e: e.collective_compute(
                kind, ALU.bypass, replica_groups=[list(range(NCORES))],
                ins=[in_ap.opt()], outs=[out_ap.opt()]))
            prog.new_phase()
            F.emit_block(prog)

        section(sec_A, {"xr": r16(x_ext), "war": wactT.rearrange("(kt p) c -> p kt c", p=128),
                        "wrr": wrecT.rearrange("(kt p) c -> p kt c", p=128),
                        "xvr": r16(xvT), "r4r": r16(r4T)})
        cc("AllToAll", xvT, xv_rx)
        xv_v = xv_rx.rearrange("(r j two) t -> j two r t", two=2, j=128)
        kv_v = kv_tx.rearrange("(r j two) t -> j two r t", two=2, j=128)
        section(sec_B, {"re3": xv_v[:, 0, :, :], "im3": xv_v[:, 1, :, :],
                        "kvre3": kv_v[:, 0, :, :], "kvim3": kv_v[:, 1, :, :]})
        cc("AllToAll", kv_tx, kv_rx)
        section(sec_C, {"xrr": r16(x_ext)[:, :, 1:TOK + 1], "kvr": r16(kv_rx),
                        "r4r": r16(r4T), "wvr": wvalT.rearrange("(kt p) c -> p kt c", p=128),
                        "ones_d": ones_d, "x1r": r16(x1T)})
        section(sec_D1, {"x1r": r16(x1T), "wgr": wgT.rearrange("(kt p) f -> p kt f", p=128),
                         "wur": wuT0.rearrange("(kt p) f -> p kt f", p=128),
                         "wdr": wdT.rearrange("(ft p) d -> p ft d", p=128),
                         "gbr": gbT.rearrange("(ft p) t -> p ft t", p=128),
                         "ubr": ubT.rearrange("(ft p) t -> p ft t", p=128),
                         "ybr": r16(ybT)})
        section(sec_D2, {"x1r": r16(x1T), "gbr": gbT.rearrange("(ft p) t -> p ft t", p=128),
                         "ubr": ubT.rearrange("(ft p) t -> p ft t", p=128),
                         "ybr": r16(ybT), "wrr": wrT.rearrange("(kt p) m -> p kt m", p=128),
                         "agr": agT.rearrange("(ekt p) m -> p ekt m", p=128),
                         "aur": auT.rearrange("(ekt p) m -> p ekt m", p=128),
                         "bgr": bgT.rearrange("(e p) f -> p e f", p=16),
                         "bur": buT.rearrange("(e p) f -> p e f", p=16),
                         "adr": adT.rearrange("(e ft p) m -> p e ft m", p=128, ft=NFF0),
                         "bdr": bdT.rearrange("(e p) d -> p e d", p=16),
                         "sel_d": sel_d, "ident_d": ident_d, "ones_d": ones_d,
                         "x2r": r16(x2T), "bnd": bnd})
        cc("AllGather", bnd, bnd_g)
        section(sec_E, {"x2r": r16(x2T), "wur": wuT1.rearrange("(kt p) c -> p kt c", p=128),
                        "wvr": wvT1.rearrange("(kt p) c -> p kt c", p=128),
                        "wg1r": wg1T.rearrange("(kt p) c -> p kt c", p=128),
                        "wg2r": wg2T.rearrange("(kt p) c -> p kt c", p=128),
                        "bu_pt": bu_pt, "bv_pt": bv_pt, "bg1_pt": bg1_pt,
                        "bg2_pt": bg2_pt, "cw_pt": cw_pt, "cb_pt": cb_pt,
                        "bnd_g": bnd_g, "oh": oh_d,
                        "tor": r16(toT), "ugr": r16(ugT), "vgr": r16(vgT)})
        cc("AllToAll", ugT, ug_rx)
        cc("AllToAll", vgT, vg_rx)
        ug_v = ug_rx.rearrange("(r h p) t -> p h r t", h=2, p=128)
        vg_v = vg_rx.rearrange("(r h p) t -> p h r t", h=2, p=128)
        go_v = go_tx.rearrange("(r h p) t -> p h r t", h=2, p=128)
        section(sec_F, {"ug3": lambda h: ug_v[:, h, :, :],
                        "vg3": lambda h: vg_v[:, h, :, :],
                        "go3": lambda h: go_v[:, h, :, :]})
        cc("AllToAll", go_tx, go_rx)
        section(sec_G, {"x2r": r16(x2T), "tor": r16(toT), "gor": r16(go_rx),
                        "wur": wuT1.rearrange("(kt p) c -> p kt c", p=128),
                        "ones_d": ones_d, "x3r": r16(x3T)})
        section(sec_H, {"x3r": r16(x3T), "su_ug": su_ug, "su_dp": su_dp,
                        "wsc_up": wsc_up, "wsc_gt": wsc_gt, "wsc_dp": wsc_dp,
                        "wupr": wupT.rearrange("(kt p) f -> p kt f", p=128),
                        "wgtr": wgtT.rearrange("(kt p) f -> p kt f", p=128),
                        "wdnr": wdnT.rearrange("(ft p) d -> p ft d", p=128),
                        "h128_d": h128_d, "ones_d": ones_d, "x4r": r16(x4T)})
    return nc


def _ct(a):
    return np.ascontiguousarray(a)


def _pt(v):
    return _ct(v.reshape(-1, 128).T)


def _maps(inp):
    f32 = np.float32
    x = np.asarray(inp["L0_attn_x"], f32)[0]
    xT = _ct(x.T)
    ones = np.ones((128, 128), f32)
    ident = np.eye(128, dtype=f32)
    sel = np.zeros((8, 128), f32)
    for e_ in range(NE):
        sel[e_, e_ * 16:(e_ + 1) * 16] = 2.0
    h128 = np.ones((1, 1), f32)
    while h128.shape[0] < 128:
        h128 = np.block([[h128, h128], [h128, -h128]])
    h128 = _ct(h128.astype(f32))
    wsug = np.asarray(inp["L1_ffn_Wscale_ug"], f32)
    xT_pad = np.concatenate([np.zeros((D, 1), f32), xT], axis=1)

    common = {
        "wactT": _ct(np.asarray(inp["L0_attn_W_act"], f32).T),
        "wrecT": _ct(np.asarray(inp["L0_attn_W_rec"], f32).T),
        "wvalT": _ct(np.asarray(inp["L0_attn_W_val"], f32).T),
        "wgT": _ct(np.asarray(inp["L0_ffn_Wg"], f32).T),
        "wuT0": _ct(np.asarray(inp["L0_ffn_Wu"], f32).T),
        "wdT": _ct(np.asarray(inp["L0_ffn_Wd"], f32).T),
        "wrT": _ct(np.asarray(inp["L0_ffn_Wr"], f32).T),
        "agT": _ct(np.asarray(inp["L0_ffn_Ag"], f32).transpose(0, 2, 1).reshape(NE * D, 16)),
        "auT": _ct(np.asarray(inp["L0_ffn_Au"], f32).transpose(0, 2, 1).reshape(NE * D, 16)),
        "bgT": _ct(np.asarray(inp["L0_ffn_Bg"], f32).transpose(0, 2, 1).reshape(NE * 16, FF0)),
        "buT": _ct(np.asarray(inp["L0_ffn_Bu"], f32).transpose(0, 2, 1).reshape(NE * 16, FF0)),
        "adT": _ct(np.asarray(inp["L0_ffn_Ad"], f32).transpose(0, 2, 1).reshape(NE * FF0, 16)),
        "bdT": _ct(np.asarray(inp["L0_ffn_Bd"], f32).transpose(0, 2, 1).reshape(NE * 16, D)),
        "sel": sel, "ident": ident, "ones": ones,
        "wuT1": _ct(np.asarray(inp["L1_attn_Wu"], f32).T),
        "wvT1": _ct(np.asarray(inp["L1_attn_Wv"], f32).T),
        "wg1T": _ct(np.asarray(inp["L1_attn_Wg1"], f32).T),
        "wg2T": _ct(np.asarray(inp["L1_attn_Wg2"], f32).T),
        "bu_pt": _pt(np.asarray(inp["L1_attn_bu"], f32)),
        "bv_pt": _pt(np.asarray(inp["L1_attn_bv"], f32)),
        "bg1_pt": _pt(np.asarray(inp["L1_attn_bg1"], f32)),
        "bg2_pt": _pt(np.asarray(inp["L1_attn_bg2"], f32)),
        "cw_pt": _ct(np.asarray(inp["L1_attn_conv_w"], f32).reshape(16, 128, 3)
                     .transpose(1, 0, 2)),
        "cb_pt": _pt(np.asarray(inp["L1_attn_conv_b"], f32)),
        "su_ug": _pt(np.asarray(inp["L1_ffn_SU_ug"], f32)),
        "su_dp": _pt(np.asarray(inp["L1_ffn_SU_dp"], f32)),
        "wsc_up": _pt(wsug[:FF1]), "wsc_gt": _pt(wsug[FF1:]),
        "wsc_dp": _pt(np.asarray(inp["L1_ffn_Wscale_dp"], f32)),
        "wupT": _ct(np.asarray(inp["L1_ffn_W_up"], f32).T),
        "wgtT": _ct(np.asarray(inp["L1_ffn_W_gate"], f32).T),
        "wdnT": _ct(np.asarray(inp["L1_ffn_W_down"], f32).T),
        "h128": h128,
    }
    maps = []
    for c in range(NCORES):
        oh = np.zeros((16, 2), f32)
        if c > 0:
            oh[(c - 1) * 2 + 1, 0] = 1.0
        if c < NCORES - 1:
            oh[(c + 1) * 2 + 0, 1] = 1.0
        m = dict(common)
        m["x_ext"] = _ct(xT_pad[:, c * TOK:c * TOK + TOK + 1])
        m["oh"] = oh
        maps.append(m)
    return maps


def kernel(**inp):
    maps = _maps(inp)
    nc = build_FUSED()
    res = run_bass_kernel_spmd(nc, maps, core_ids=list(range(NCORES)))
    x4 = np.concatenate([r["x4T"] for r in res.results], axis=1)
    return _ct(x4.T).reshape(1, T, D)


# version 04
# speedup vs baseline: 1.0286x; 1.0286x over previous
import numpy as np
from contextlib import ExitStack

import concourse.bass as bass
import concourse.mybir as mybir
from concourse.bass_utils import run_bass_kernel_spmd
from concourse.aot_env import allow_ici_all_to_all

F32 = mybir.dt.float32
BF16 = mybir.dt.bfloat16
AF = mybir.ActivationFunctionType
ALU = mybir.AluOpType
AX = mybir.AxisListType

NCORES = 8
D = 2048
T = 8192
TOK = T // NCORES
NT = 512
NG = TOK // NT
DKT = D // 128
FF0 = 5632
NFF0 = FF0 // 128
FF1 = 8192
NFF1 = FF1 // 128
NE = 8
EPS = 1e-5

ENGINES = ("tensor", "vector", "scalar", "gpsimd", "sync")


class Prog:
    def __init__(self):
        self.phases = []
        self.new_phase()

    def new_phase(self):
        self.cur = {e: [] for e in ENGINES}
        self.phases.append(self.cur)

    def op(self, eng, fn):
        self.cur[eng].append(("c", fn))

    def dma(self, eng, fn):
        self.cur[eng].append(("d", fn))

    def cc(self, fn):
        self.cur["gpsimd"].append(("cc", fn))


class Fused:

    def __init__(self, nc, st):
        self.nc = nc
        self.st = st
        self.sems = {}
        for e in ENGINES:
            self.sems["c:" + e] = st.enter_context(nc.semaphore("sc_" + e))
            self.sems["d:" + e] = st.enter_context(nc.semaphore("sd_" + e))
        self.cum = {k: 0 for k in self.sems}
        self.ncc = 0
        self.nblk = 0

    def emit_block(self, prog):
        nc = self.nc
        cc_map = {}
        for pi, ph in enumerate(prog.phases):
            for ei, (kind, fn) in enumerate(ph["gpsimd"]):
                if kind == "cc":
                    key = "cc%d" % self.ncc
                    self.ncc += 1
                    cc_map[(pi, ei)] = key
                    self.sems[key] = self.st.enter_context(nc.semaphore(key))
                    self.cum[key] = 0
        base = dict(self.cum)
        for k in self.sems:
            base.setdefault(k, 0)
        totals = []
        cum = dict(base)
        for ph in prog.phases:
            for e in ENGINES:
                for kind, _fn in ph[e]:
                    if kind == "c":
                        cum["c:" + e] += 1
                    elif kind == "d":
                        cum["d:" + e] += 16
            for ei, (kind, _fn) in enumerate(ph["gpsimd"]):
                pass
            totals.append(dict(cum))
        for (pi, ei), key in cc_map.items():
            for pj in range(pi, len(totals)):
                totals[pj][key] = 1
        final = totals[-1] if totals else base
        self.cum = dict(final)

        sems = self.sems

        def body_for(e):
            def body(eng):
                last = {k: 0 for k in sems}
                started = False
                for pi, ph in enumerate(prog.phases):
                    ops = ph[e]
                    if not ops:
                        continue
                    tgt = base if (pi == 0 or not started) else totals[pi - 1]
                    if pi > 0 and started:
                        tgt = totals[pi - 1]
                    else:
                        tgt = dict(base)
                        if pi > 0:
                            tgt.update(totals[pi - 1])
                    for k, v in tgt.items():
                        if v > last.get(k, 0):
                            eng.wait_ge(sems[k], v)
                            last[k] = v
                    started = True
                    for ei, (kind, fn) in enumerate(ops):
                        if kind == "cc":
                            with allow_ici_all_to_all():
                                inst = fn(eng)
                            inst.then_inc(sems[cc_map[(pi, ei)]])
                        else:
                            inst = fn(eng)
                            inst.then_inc(sems[("c:" if kind == "c" else "d:") + e],
                                          1 if kind == "c" else 16)
                for k, v in final.items():
                    if v > last.get(k, 0):
                        eng.wait_ge(sems[k], v)
                        last[k] = v
            return body

        self.nblk += 1
        with self.nc.Block("blk%d" % self.nblk) as blk:
            blk.tensor(body_for("tensor"))
            blk.vector(body_for("vector"))
            blk.scalar(body_for("scalar"))
            blk.gpsimd(body_for("gpsimd"))
            blk.sync(body_for("sync"))


def ln_group(prog, c, y_ap, out_write, nkt=DKT):
    ones = c["ones"]
    s1, s2, pA, pB = c["pb_s1"], c["pb_s2"], c["pb_A"], c["pb_B"]
    stats = c["stats"]
    sq = c["sq"]
    inv_d = 1.0 / (128.0 * nkt)

    for kt in range(nkt):
        prog.op("tensor", lambda e, kt=kt: e.matmul(
            s1[0:1, :], ones[:, 0:1], y_ap(kt), start=(kt == 0), stop=(kt == nkt - 1)))
    prog.new_phase()
    for kt in range(nkt + 1):
        if kt < nkt:
            prog.op("scalar", lambda e, kt=kt: e.activation(
                sq[kt % 2][:, :], y_ap(kt), AF.Square))
        if kt > 0:
            prog.op("tensor", lambda e, kt=kt: e.matmul(
                s2[0:1, :], ones[:, 0:1], sq[(kt - 1) % 2][:, :],
                start=(kt - 1 == 0), stop=(kt - 1 == nkt - 1)))
        prog.new_phase()
    mu = stats[:, 0 * NT:1 * NT]
    ex2 = stats[:, 1 * NT:2 * NT]
    tmp = stats[:, 2 * NT:3 * NT]
    rstd = stats[:, 3 * NT:4 * NT]
    prog.op("scalar", lambda e: e.activation(mu, s1[0:1, :], AF.Identity, scale=inv_d))
    prog.op("vector", lambda e: e.tensor_scalar_mul(ex2, s2[0:1, :], inv_d))
    prog.new_phase()
    prog.op("scalar", lambda e: e.activation(tmp, mu, AF.Square))
    prog.new_phase()
    prog.op("vector", lambda e: e.tensor_sub(ex2, ex2, tmp))
    prog.new_phase()
    prog.op("scalar", lambda e: e.activation(tmp, ex2, AF.Sqrt, bias=c["eps"]))
    prog.new_phase()
    prog.op("vector", lambda e: e.reciprocal(rstd, tmp))
    prog.new_phase()
    prog.op("vector", lambda e: e.tensor_mul(mu, mu, rstd))
    prog.new_phase()
    prog.op("tensor", lambda e: e.matmul(pA[:, :], ones[0:1, :], rstd, start=True, stop=True))
    prog.op("tensor", lambda e: e.matmul(pB[:, :], ones[0:1, :], mu, start=True, stop=True))
    prog.new_phase()
    pending = None
    for kt in range(nkt + 2):
        if pending is not None:
            pending(prog)
            pending = None
        if kt < nkt:
            prog.op("vector", lambda e, kt=kt: e.tensor_mul(sq[kt % 2][:, :], y_ap(kt), pA[:, :]))
        if 0 < kt <= nkt:
            dst, post = out_write(kt - 1)
            prog.op("vector", lambda e, kt=kt, dst=dst: e.tensor_sub(
                dst, sq[(kt - 1) % 2][:, :], pB[:, :]))
            pending = post
        prog.new_phase()


def sec_A(prog, sb, ps, io):
    xr, war, wrr, xvr, r4r = io["xr"], io["war"], io["wrr"], io["xvr"], io["r4r"]
    stage = sb("aA_stage", [128, TOK + 1], F32)
    xbf = sb("aA_xbf", [128, DKT, TOK + 1], BF16)
    wsl = sb("aA_wsl", [128, 32, 128], F32)
    wbf = sb("aA_wbf", [128, 32, 128], BF16)
    outsl = sb("aA_outsl", [128, DKT, TOK], F32)
    pb = [ps("aA_pb%d" % i) for i in range(2)]

    for kt in range(DKT):
        prog.dma("sync", lambda e, kt=kt: e.dma_start(out=stage[:, :], in_=xr[:, kt, :]))
        prog.new_phase()
        prog.op("scalar", lambda e, kt=kt: e.copy(out=xbf[:, kt, :], in_=stage[:, :]))
        prog.new_phase()

    def rhs_for(k, g):
        if k < DKT:
            return xbf[:, k, g * 512:(g + 1) * 512]
        return xbf[:, k - DKT, 1 + g * 512:1 + (g + 1) * 512]

    def wpass(wr_ap, out_r, square_twice):
        for ct in range(DKT):
            for hh in range(2):
                prog.dma("sync", lambda e, ct=ct, hh=hh: e.dma_start(
                    out=wsl[:, hh * 16:(hh + 1) * 16, :],
                    in_=wr_ap[:, hh * 16:(hh + 1) * 16, ct * 128:(ct + 1) * 128]))
            prog.new_phase()
            prog.op("scalar", lambda e: e.copy(
                out=wbf[:, :, :].rearrange("p a b -> p (a b)"),
                in_=wsl[:, :, :].rearrange("p a b -> p (a b)")))
            prog.new_phase()
            for g in range(2):
                for k in range(32):
                    prog.op("tensor", lambda e, g=g, k=k: e.matmul(
                        pb[g][:, :], wbf[:, k, :], rhs_for(k, g),
                        start=(k == 0), stop=(k == 31)))
            prog.new_phase()
            for g in range(2):
                prog.op("scalar", lambda e, ct=ct, g=g: e.activation(
                    outsl[:, ct, g * 512:(g + 1) * 512], pb[g][:, :], AF.Sigmoid))
            prog.new_phase()
            for g in range(2):
                prog.op("vector", lambda e, ct=ct, g=g: e.tensor_scalar_max(
                    outsl[:, ct, g * 512:(g + 1) * 512],
                    outsl[:, ct, g * 512:(g + 1) * 512], 0.5))
            prog.new_phase()
            if square_twice:
                for _ in range(2):
                    prog.op("scalar", lambda e, ct=ct: e.activation(
                        outsl[:, ct, :], outsl[:, ct, :], AF.Square))
                    prog.new_phase()
        for ch in range(4):
            prog.dma("gpsimd", lambda e, ch=ch: e.dma_start(
                out=out_r[:, ch * 4:(ch + 1) * 4, :],
                in_=outsl[:, ch * 4:(ch + 1) * 4, :]))
        prog.new_phase()

    wpass(war, xvr, False)
    wpass(wrr, r4r, True)


def sec_B(prog, sb, ps, io):
    re3, im3, kvre3, kvim3 = io["re3"], io["im3"], io["kvre3"], io["kvim3"]
    rA, iA, rB, iB, tmp = (sb("aB_" + n, [128, T], F32)
                           for n in ("rA", "iA", "rB", "iB", "tmp"))
    v3 = lambda t: t[:, :].rearrange("p (r t2) -> p r t2", r=8)
    prog.dma("sync", lambda e: e.dma_start(out=v3(rA), in_=re3))
    prog.dma("sync", lambda e: e.dma_start(out=v3(iA), in_=im3))
    prog.new_phase()
    prog.op("vector", lambda e: e.tensor_mul(tmp[:, :], rA[:, :], rA[:, :]))
    prog.op("gpsimd", lambda e: e.tensor_mul(rB[:, :], iA[:, :], iA[:, :]))
    prog.new_phase()
    prog.op("vector", lambda e: e.tensor_add(tmp[:, :], tmp[:, :], rB[:, :]))
    prog.new_phase()
    prog.op("scalar", lambda e: e.activation(tmp[:, :], tmp[:, :], AF.Sqrt))
    prog.new_phase()
    prog.op("vector", lambda e: e.tensor_tensor_scan(
        rB[:, :], tmp[:, :], tmp[:, :], 0.0, ALU.max, ALU.bypass))
    prog.new_phase()
    prog.op("vector", lambda e: e.reciprocal(tmp[:, :], rB[:, :]))
    prog.new_phase()
    prog.op("vector", lambda e: e.tensor_mul(rA[:, :], rA[:, :], tmp[:, :]))
    prog.op("gpsimd", lambda e: e.tensor_mul(iA[:, :], iA[:, :], tmp[:, :]))
    prog.new_phase()
    bufs = [rA, iA, rB, iB]
    k = 1
    while k < T:
        ra, ia, rb, ib = bufs
        prog.op("vector", lambda e, k=k, ra=ra, rb=rb: e.tensor_mul(
            rb[:, k:], ra[:, k:], ra[:, :T - k]))
        prog.op("gpsimd", lambda e, k=k, ra=ra, ia=ia, ib=ib: e.tensor_mul(
            ib[:, k:], ra[:, k:], ia[:, :T - k]))
        prog.op("scalar", lambda e, k=k, ra=ra, rb=rb: e.copy(
            out=rb[:, :k], in_=ra[:, :k]))
        prog.new_phase()
        prog.op("vector", lambda e, k=k, ia=ia: e.tensor_mul(
            tmp[:, k:], ia[:, k:], ia[:, :T - k]))
        prog.op("scalar", lambda e, k=k, ia=ia, ib=ib: e.copy(
            out=ib[:, :k], in_=ia[:, :k]))
        prog.new_phase()
        prog.op("vector", lambda e, k=k, rb=rb: e.tensor_sub(
            rb[:, k:], rb[:, k:], tmp[:, k:]))
        prog.new_phase()
        prog.op("vector", lambda e, k=k, ia=ia, ra=ra: e.tensor_mul(
            tmp[:, k:], ia[:, k:], ra[:, :T - k]))
        prog.new_phase()
        prog.op("vector", lambda e, k=k, ib=ib: e.tensor_add(
            ib[:, k:], ib[:, k:], tmp[:, k:]))
        prog.new_phase()
        bufs = [rb, ib, ra, ia]
        k *= 2
    ra, ia = bufs[0], bufs[1]
    prog.dma("gpsimd", lambda e, ra=ra: e.dma_start(out=kvre3, in_=v3(ra)))
    prog.dma("gpsimd", lambda e, ia=ia: e.dma_start(out=kvim3, in_=v3(ia)))
    prog.new_phase()


def sec_C(prog, sb, ps, io):
    xrr, kvr, r4r, wvr, ones_d, x1r = (io["xrr"], io["kvr"], io["r4r"],
                                       io["wvr"], io["ones_d"], io["x1r"])
    xsl = sb("aC_xsl", [128, DKT, TOK], F32)
    r4sl = sb("aC_r4sl", [128, DKT, TOK], F32)
    kvbf = sb("aC_kvbf", [128, DKT, TOK], BF16)
    stage = sb("aC_stage", [128, TOK], F32)
    wsl = sb("aC_wsl", [128, 8, 128], F32)
    wbf = sb("aC_wbf", [128, DKT, 128], BF16)
    scr = sb("aC_scr", [128, 2, 512], F32)
    sq = [sb("aC_sq0", [128, NT], F32), sb("aC_sq1", [128, NT], F32)]
    stats = sb("aC_stats", [1, 4 * NT], F32)
    ones = sb("aC_ones", [128, 128], F32)
    eps = sb("aC_eps", [1, 1], F32)
    pb = [ps("aC_pb%d" % i) for i in range(6)]

    prog.op("vector", lambda e: e.memset(eps[:, :], EPS))
    for ch in range(4):
        prog.dma("sync", lambda e, ch=ch: e.dma_start(
            out=xsl[:, ch * 4:(ch + 1) * 4, :], in_=xrr[:, ch * 4:(ch + 1) * 4, :]))
        prog.dma("sync", lambda e, ch=ch: e.dma_start(
            out=r4sl[:, ch * 4:(ch + 1) * 4, :], in_=r4r[:, ch * 4:(ch + 1) * 4, :]))
    prog.dma("sync", lambda e: e.dma_start(out=ones[:, :], in_=ones_d[:, :]))
    prog.new_phase()
    for kt in range(DKT):
        prog.dma("sync", lambda e, kt=kt: e.dma_start(out=stage[:, :], in_=kvr[:, kt, :]))
        prog.new_phase()
        prog.op("scalar", lambda e, kt=kt: e.copy(out=kvbf[:, kt, :], in_=stage[:, :]))
        prog.new_phase()
    for ct in range(DKT):
        for hh in range(2):
            prog.dma("sync", lambda e, ct=ct, hh=hh: e.dma_start(
                out=wsl[:, :, :],
                in_=wvr[:, hh * 8:(hh + 1) * 8, ct * 128:(ct + 1) * 128]))
            prog.new_phase()
            prog.op("scalar", lambda e, hh=hh: e.copy(
                out=wbf[:, hh * 8:(hh + 1) * 8, :].rearrange("p a b -> p (a b)"),
                in_=wsl[:, :, :].rearrange("p a b -> p (a b)")))
            prog.new_phase()
        for g in range(2):
            for k in range(DKT):
                prog.op("tensor", lambda e, g=g, k=k: e.matmul(
                    pb[g][:, :], wbf[:, k, :], kvbf[:, k, g * 512:(g + 1) * 512],
                    start=(k == 0), stop=(k == DKT - 1)))
        prog.new_phase()
        for g in range(2):
            prog.op("vector", lambda e, ct=ct, g=g: e.tensor_mul(
                scr[:, g, :], pb[g][:, :], r4sl[:, ct, g * 512:(g + 1) * 512]))
        prog.new_phase()
        for g in range(2):
            prog.op("gpsimd", lambda e, ct=ct, g=g: e.tensor_add(
                xsl[:, ct, g * 512:(g + 1) * 512],
                xsl[:, ct, g * 512:(g + 1) * 512], scr[:, g, :]))
        prog.new_phase()
    lnc = {"ones": ones, "pb_s1": pb[2], "pb_s2": pb[3], "pb_A": pb[4], "pb_B": pb[5],
           "stats": stats, "sq": sq, "eps": eps[0:1, 0:1]}
    for g in range(NG):
        def out_write(kt, g=g):
            return r4sl[:, kt, g * NT:(g + 1) * NT], None
        ln_group(prog, lnc, lambda kt, g=g: xsl[:, kt, g * NT:(g + 1) * NT], out_write)
    for ch in range(4):
        prog.dma("gpsimd", lambda e, ch=ch: e.dma_start(
            out=x1r[:, ch * 4:(ch + 1) * 4, :], in_=r4sl[:, ch * 4:(ch + 1) * 4, :]))
    prog.new_phase()


def sec_D1(prog, sb, ps, io):
    x1r, wgr, wur, wdr, gbr, ubr, ybr = (io["x1r"], io["wgr"], io["wur"], io["wdr"],
                                         io["gbr"], io["ubr"], io["ybr"])
    x1bf = sb("aD_x1bf", [128, DKT, TOK], BF16)
    stage = sb("aD_stage", [128, DKT, 128], F32)
    wgbf = sb("aD_wgbf", [128, DKT, 128], BF16)
    wubf = sb("aD_wubf", [128, DKT, 128], BF16)
    wdbf = sb("aD_wdbf", [128, NFF0, 128], BF16)
    base_h = sb("aD_base_h", [128, NFF0, NT], BF16)
    gb = sb("aD_gb", [128, 2, NT], F32)
    ub = sb("aD_ub", [128, 2, NT], F32)
    silu_t = sb("aD_silu", [128, NT], F32)
    xres = sb("aD_xres", [128, NT], F32)
    oscr = sb("aD_oscr", [128, NT], F32)
    pb = [ps("aD_pb%d" % i) for i in range(3)]

    for qq in range(4):
        for half, csl in ((0, slice(0, NT)), (1, slice(NT, TOK))):
            for kk in range(4):
                prog.dma("sync", lambda e, qq=qq, kk=kk, csl=csl: e.dma_start(
                    out=stage[:, kk * 4:(kk + 1) * 4, :].rearrange("p a b -> p (a b)"),
                    in_=x1r[:, qq * 4 + kk, csl]))
            prog.new_phase()
            prog.op("scalar", lambda e, qq=qq, csl=csl: e.copy(
                out=x1bf[:, qq * 4:(qq + 1) * 4, csl],
                in_=stage[:, :, :].rearrange("p (c dd) b -> p c (dd b)", c=4)))
            prog.new_phase()

    for g in range(NG):
        gs = slice(g * NT, (g + 1) * NT)
        for ft in range(NFF0):
            fs = slice(ft * 128, (ft + 1) * 128)
            prog.dma("sync", lambda e, fs=fs: e.dma_start(out=stage[:, :, :],
                                                          in_=wgr[:, :, fs]))
            prog.new_phase()
            prog.op("scalar", lambda e: e.copy(
                out=wgbf[:, :, :].rearrange("p a b -> p (a b)"),
                in_=stage[:, :, :].rearrange("p a b -> p (a b)")))
            prog.new_phase()
            prog.dma("sync", lambda e, fs=fs: e.dma_start(out=stage[:, :, :],
                                                          in_=wur[:, :, fs]))
            for k in range(DKT):
                prog.op("tensor", lambda e, k=k, gs=gs: e.matmul(
                    pb[0][:, :], wgbf[:, k, :], x1bf[:, k, gs],
                    start=(k == 0), stop=(k == DKT - 1)))
            prog.new_phase()
            prog.op("scalar", lambda e: e.copy(
                out=wubf[:, :, :].rearrange("p a b -> p (a b)"),
                in_=stage[:, :, :].rearrange("p a b -> p (a b)")))
            prog.op("vector", lambda e: e.tensor_copy(out=gb[:, 0, :], in_=pb[0][:, :]))
            prog.new_phase()
            for k in range(DKT):
                prog.op("tensor", lambda e, k=k, gs=gs: e.matmul(
                    pb[1][:, :], wubf[:, k, :], x1bf[:, k, gs],
                    start=(k == 0), stop=(k == DKT - 1)))
            prog.op("scalar", lambda e: e.activation(silu_t[:, :], gb[:, 0, :], AF.Silu))
            prog.new_phase()
            prog.op("vector", lambda e: e.tensor_copy(out=ub[:, 0, :], in_=pb[1][:, :]))
            prog.new_phase()
            prog.op("vector", lambda e, ft=ft: e.tensor_mul(
                base_h[:, ft, :], silu_t[:, :], ub[:, 0, :]))
            prog.dma("gpsimd", lambda e, ft=ft, gs=gs: e.dma_start(
                out=gbr[:, ft, gs], in_=gb[:, 0, :]))
            prog.dma("gpsimd", lambda e, ft=ft, gs=gs: e.dma_start(
                out=ubr[:, ft, gs], in_=ub[:, 0, :]))
            prog.new_phase()
        for dt in range(DKT):
            ds = slice(dt * 128, (dt + 1) * 128)
            for ch in range(6):
                ftsl = slice(ch * 8, min((ch + 1) * 8, NFF0))
                n = ftsl.stop - ftsl.start
                prog.dma("sync", lambda e, ds=ds, ftsl=ftsl, n=n: e.dma_start(
                    out=stage[:, 0:n, :], in_=wdr[:, ftsl, ds]))
                prog.new_phase()
                prog.op("scalar", lambda e, ftsl=ftsl, n=n: e.copy(
                    out=wdbf[:, ftsl, :].rearrange("p a b -> p (a b)"),
                    in_=stage[:, 0:n, :].rearrange("p a b -> p (a b)")))
                prog.new_phase()
            prog.dma("sync", lambda e, dt=dt, gs=gs: e.dma_start(
                out=xres[:, :], in_=x1r[:, dt, gs]))
            prog.new_phase()
            for ft in range(NFF0):
                prog.op("tensor", lambda e, ft=ft: e.matmul(
                    pb[2][:, :], wdbf[:, ft, :], base_h[:, ft, :],
                    start=(ft == 0), stop=(ft == NFF0 - 1)))
            prog.new_phase()
            prog.op("vector", lambda e: e.tensor_add(oscr[:, :], pb[2][:, :], xres[:, :]))
            prog.new_phase()
            prog.dma("gpsimd", lambda e, dt=dt, gs=gs: e.dma_start(
                out=ybr[:, dt, gs], in_=oscr[:, :]))
            prog.new_phase()


def sec_D2(prog, sb, ps, io):
    (x1r, gbr, ubr, ybr, wrr, agr, aur, bgr, bur, adr, bdr,
     sel_d, ident_d, ones_d, x2r, bnd) = (
        io["x1r"], io["gbr"], io["ubr"], io["ybr"], io["wrr"], io["agr"], io["aur"],
        io["bgr"], io["bur"], io["adr"], io["bdr"], io["sel_d"], io["ident_d"],
        io["ones_d"], io["x2r"], io.get("bnd"))
    x1bf = sb("a2_x1bf", [128, DKT, NT], BF16)
    ybuf = sb("a2_ybuf", [128, DKT, NT], F32)
    stage = sb("a2_stage", [128, 4, 128], F32)
    wrbf = sb("a2_wrbf", [128, DKT, 8], BF16)
    agbf = sb("a2_agbf", [128, DKT, 16], BF16)
    bst = sb("a2_bst", [16, NE, 128], F32)
    bgbf = sb("a2_bgbf", [16, NE, 128], BF16)
    bubf = sb("a2_bubf", [16, NE, 128], BF16)
    adst = sb("a2_adst", [128, NE, 16], F32)
    adbf = sb("a2_adbf", [128, NE, 16], BF16)
    bdbf = sb("a2_bdbf", [16, NE, 128], BF16)
    tg_bf = sb("a2_tg", [16, NE, NT], BF16)
    tu_bf = sb("a2_tu", [16, NE, NT], BF16)
    s_acc = sb("a2_sacc", [16, NE, NT], F32)
    s_bf = sb("a2_sbf", [16, NE, NT], BF16)
    we8 = sb("a2_we8", [8, NT], F32)
    probs = sb("a2_probs", [128, 32], F32)
    probs2 = sb("a2_probs2", [128, 32], F32)
    misc = sb("a2_misc", [128, 16], F32)
    gb = sb("a2_gb", [128, NT], F32)
    ub = sb("a2_ub", [128, NT], F32)
    hg = sb("a2_hg", [128, NT], F32)
    hu = sb("a2_hu", [128, NT], F32)
    hbf = sb("a2_hbf", [128, NT], BF16)
    xres = sb("a2_xres", [128, NT], F32)
    sq = [sb("a2_sq0", [128, NT], F32), sb("a2_sq1", [128, NT], F32)]
    stats = sb("a2_stats", [1, 4 * NT], F32)
    ones = sb("a2_ones", [128, 128], F32)
    ident = sb("a2_ident", [128, 128], F32)
    sel = sb("a2_sel", [8, 128], F32)
    outt = sb("a2_outt", [128, 2, NT], F32)
    eps = sb("a2_eps", [1, 1], F32)
    pb = [ps("a2_pb%d" % i) for i in range(8)]

    prog.op("vector", lambda e: e.memset(eps[:, :], EPS))
    prog.dma("sync", lambda e: e.dma_start(out=ones[:, :], in_=ones_d[:, :]))
    prog.dma("sync", lambda e: e.dma_start(out=ident[:, :], in_=ident_d[:, :]))
    prog.dma("sync", lambda e: e.dma_start(out=sel[:, :], in_=sel_d[:, :]))
    prog.dma("sync", lambda e: e.dma_start(
        out=stage[:, 0, :].rearrange("p (a m) -> p a m", m=8), in_=wrr[:, :, :]))
    prog.new_phase()
    prog.op("scalar", lambda e: e.copy(
        out=wrbf[:, :, :].rearrange("p a m -> p (a m)"), in_=stage[:, 0, :]))
    prog.new_phase()

    for g in range(NG):
        gs = slice(g * NT, (g + 1) * NT)
        for kt in range(DKT):
            prog.dma("sync", lambda e, kt=kt, gs=gs: e.dma_start(
                out=sq[0][:, :], in_=x1r[:, kt, gs]))
            prog.new_phase()
            prog.op("scalar", lambda e, kt=kt: e.copy(out=x1bf[:, kt, :], in_=sq[0][:, :]))
            prog.new_phase()
        for sub in range(4):
            for k in range(DKT):
                prog.op("tensor", lambda e, sub=sub, k=k: e.matmul(
                    pb[sub][:, 0:8], x1bf[:, k, sub * 128:(sub + 1) * 128],
                    wrbf[:, k, :], start=(k == 0), stop=(k == DKT - 1)))
        prog.new_phase()
        for sub in range(4):
            prog.op("vector", lambda e, sub=sub: e.reduce_max(
                misc[:, sub:sub + 1], pb[sub][:, 0:8], AX.X))
        prog.new_phase()
        for sub in range(4):
            prog.op("vector", lambda e, sub=sub: e.tensor_scalar(
                probs[:, sub * 8:(sub + 1) * 8], pb[sub][:, 0:8],
                misc[:, sub:sub + 1], None, ALU.subtract))
        prog.new_phase()
        prog.op("scalar", lambda e: e.activation(probs[:, :], probs[:, :], AF.Exp))
        prog.new_phase()
        for sub in range(4):
            prog.op("vector", lambda e, sub=sub: e.reduce_sum(
                misc[:, 4 + sub:5 + sub], probs[:, sub * 8:(sub + 1) * 8], AX.X))
        prog.new_phase()
        prog.op("vector", lambda e: e.reciprocal(misc[:, 4:8], misc[:, 4:8]))
        prog.new_phase()
        for sub in range(4):
            prog.op("vector", lambda e, sub=sub: e.tensor_scalar_mul(
                probs[:, sub * 8:(sub + 1) * 8], probs[:, sub * 8:(sub + 1) * 8],
                misc[:, 4 + sub:5 + sub]))
        prog.new_phase()
        for sub in range(4):
            prog.op("vector", lambda e, sub=sub: e.reduce_max(
                misc[:, sub:sub + 1], probs[:, sub * 8:(sub + 1) * 8], AX.X))
        prog.new_phase()
        for sub in range(4):
            prog.op("vector", lambda e, sub=sub: e.tensor_scalar(
                probs2[:, sub * 8:(sub + 1) * 8], probs[:, sub * 8:(sub + 1) * 8],
                misc[:, sub:sub + 1], -1.0e9, ALU.is_ge, ALU.mult))
        prog.new_phase()
        prog.op("vector", lambda e: e.tensor_add(probs2[:, :], probs2[:, :], probs[:, :]))
        prog.new_phase()
        for sub in range(4):
            prog.op("vector", lambda e, sub=sub: e.reduce_max(
                misc[:, 8 + sub:9 + sub], probs2[:, sub * 8:(sub + 1) * 8], AX.X))
        prog.new_phase()
        for sub in range(4):
            prog.op("vector", lambda e, sub=sub: e.tensor_scalar(
                probs2[:, sub * 8:(sub + 1) * 8], probs[:, sub * 8:(sub + 1) * 8],
                misc[:, 8 + sub:9 + sub], None, ALU.is_ge))
        prog.new_phase()
        prog.op("vector", lambda e: e.tensor_mul(probs[:, :], probs[:, :], probs2[:, :]))
        prog.new_phase()
        for sub in range(4):
            prog.op("tensor", lambda e, sub=sub: e.matmul(
                pb[4][0:8, sub * 128:(sub + 1) * 128], probs[:, sub * 8:(sub + 1) * 8],
                ident[:, :], start=(sub == 0), stop=(sub == 3), skip_group_check=True))
        prog.new_phase()
        prog.op("vector", lambda e: e.tensor_copy(out=we8[:, :], in_=pb[4][0:8, :]))
        prog.new_phase()
        for e_ in range(NE):
            for w_ in range(2):
                src = agr if w_ == 0 else aur
                dst = tg_bf if w_ == 0 else tu_bf
                prog.dma("sync", lambda e, e_=e_, src=src: e.dma_start(
                    out=stage[:, 0:2, :].rearrange("p a b -> p (a b)")
                        .rearrange("p (a m) -> p a m", m=16),
                    in_=src[:, e_ * 16:(e_ + 1) * 16, :]))
                prog.new_phase()
                prog.op("scalar", lambda e: e.copy(
                    out=agbf[:, :, :].rearrange("p a m -> p (a m)"),
                    in_=stage[:, 0:2, :].rearrange("p a b -> p (a b)")))
                prog.new_phase()
                for k in range(DKT):
                    prog.op("tensor", lambda e, k=k: e.matmul(
                        pb[5][0:16, :], agbf[:, k, :], x1bf[:, k, :],
                        start=(k == 0), stop=(k == DKT - 1)))
                prog.new_phase()
                prog.op("scalar", lambda e, e_=e_, dst=dst: e.copy(
                    out=dst[:, e_, :], in_=pb[5][0:16, :]))
                prog.new_phase()
        prog.op("vector", lambda e: e.memset(s_acc[:, :, :], 0.0))
        prog.new_phase()
        for ft in range(NFF0):
            fs = slice(ft * 128, (ft + 1) * 128)
            prog.dma("sync", lambda e, ft=ft, gs=gs: e.dma_start(
                out=gb[:, :], in_=gbr[:, ft, gs]))
            prog.dma("sync", lambda e, ft=ft, gs=gs: e.dma_start(
                out=ub[:, :], in_=ubr[:, ft, gs]))
            prog.dma("sync", lambda e, fs=fs: e.dma_start(out=bst[:, :, :], in_=bgr[:, :, fs]))
            prog.dma("sync", lambda e, ft=ft: e.dma_start(out=adst[:, :, :],
                                                          in_=adr[:, :, ft, :]))
            prog.new_phase()
            prog.op("vector", lambda e: e.tensor_copy(
                out=bgbf[:, :, :].rearrange("p e f -> p (e f)"),
                in_=bst[:, :, :].rearrange("p e f -> p (e f)")))
            prog.op("scalar", lambda e: e.copy(
                out=adbf[:, :, :].rearrange("p e m -> p (e m)"),
                in_=adst[:, :, :].rearrange("p e m -> p (e m)")))
            prog.new_phase()
            prog.dma("sync", lambda e, fs=fs: e.dma_start(out=bst[:, :, :], in_=bur[:, :, fs]))
            prog.new_phase()
            prog.op("gpsimd", lambda e: e.tensor_copy(
                out=bubf[:, :, :].rearrange("p e f -> p (e f)"),
                in_=bst[:, :, :].rearrange("p e f -> p (e f)")))
            prog.new_phase()
            qS = lambda i: pb[5 + (i % 2)]
            for e_ in range(NE):
                prog.op("tensor", lambda e, e_=e_: e.matmul(
                    pb[2][:, :], bgbf[:, e_, :], tg_bf[:, e_, :], start=True, stop=True))
                prog.op("tensor", lambda e, e_=e_: e.matmul(
                    pb[3][:, :], bubf[:, e_, :], tu_bf[:, e_, :], start=True, stop=True))
                if e_ >= 1:
                    prog.op("tensor", lambda e, e_=e_: e.matmul(
                        qS(e_ - 1)[0:16, :], adbf[:, e_ - 1, :], hbf[:, :],
                        start=True, stop=True))
                if e_ >= 2:
                    prog.op("vector", lambda e, e_=e_: e.tensor_add(
                        s_acc[:, e_ - 2, :], s_acc[:, e_ - 2, :], qS(e_ - 2)[0:16, :]))
                prog.new_phase()
                prog.op("vector", lambda e: e.scalar_tensor_tensor(
                    hg[:, :], pb[2][:, :], 2.0, gb[:, :], ALU.mult, ALU.add))
                prog.new_phase()
                prog.op("vector", lambda e: e.scalar_tensor_tensor(
                    hu[:, :], pb[3][:, :], 2.0, ub[:, :], ALU.mult, ALU.add))
                prog.op("scalar", lambda e: e.activation(hg[:, :], hg[:, :], AF.Silu))
                prog.new_phase()
                prog.op("vector", lambda e: e.tensor_mul(hbf[:, :], hg[:, :], hu[:, :]))
                prog.new_phase()
            prog.op("tensor", lambda e: e.matmul(
                qS(NE - 1)[0:16, :], adbf[:, NE - 1, :], hbf[:, :], start=True, stop=True))
            prog.op("vector", lambda e: e.tensor_add(
                s_acc[:, NE - 2, :], s_acc[:, NE - 2, :], qS(NE - 2)[0:16, :]))
            prog.new_phase()
            prog.op("vector", lambda e: e.tensor_add(
                s_acc[:, NE - 1, :], s_acc[:, NE - 1, :], qS(NE - 1)[0:16, :]))
            prog.new_phase()
        for e_ in range(NE):
            prog.op("tensor", lambda e, e_=e_: e.matmul(
                pb[7][0:16, :], sel[:, e_ * 16:(e_ + 1) * 16], we8[:, :],
                start=True, stop=True))
            prog.new_phase()
            prog.op("vector", lambda e, e_=e_: e.tensor_mul(
                s_bf[:, e_, :], s_acc[:, e_, :], pb[7][0:16, :]))
            prog.new_phase()
        for dt in range(DKT):
            ds = slice(dt * 128, (dt + 1) * 128)
            prog.dma("sync", lambda e, ds=ds: e.dma_start(out=bst[:, :, :], in_=bdr[:, :, ds]))
            prog.dma("sync", lambda e, dt=dt, gs=gs: e.dma_start(
                out=xres[:, :], in_=ybr[:, dt, gs]))
            prog.new_phase()
            prog.op("scalar", lambda e: e.copy(
                out=bdbf[:, :, :].rearrange("p e f -> p (e f)"),
                in_=bst[:, :, :].rearrange("p e f -> p (e f)")))
            prog.new_phase()
            for e_ in range(NE):
                prog.op("tensor", lambda e, e_=e_: e.matmul(
                    pb[7][:, :], bdbf[:, e_, :], s_bf[:, e_, :],
                    start=(e_ == 0), stop=(e_ == NE - 1)))
            prog.new_phase()
            prog.op("vector", lambda e, dt=dt: e.tensor_add(
                ybuf[:, dt, :], pb[7][:, :], xres[:, :]))
            prog.new_phase()
        lnc = {"ones": ones, "pb_s1": pb[0], "pb_s2": pb[1], "pb_A": pb[2],
               "pb_B": pb[3], "stats": stats, "sq": sq, "eps": eps[0:1, 0:1]}

        def out_write(kt, g=g):
            dst = outt[:, kt % 2, :]

            def post(p, kt=kt, g=g):
                p.dma("gpsimd", lambda e, kt=kt, g=g: e.dma_start(
                    out=x2r[:, kt, g * NT:(g + 1) * NT], in_=outt[:, kt % 2, :]))
                if bnd is not None and g == 0:
                    p.dma("gpsimd", lambda e, kt=kt: e.dma_start(
                        out=bnd[0, kt * 128:(kt + 1) * 128].rearrange(
                            "(p one) -> p one", one=1),
                        in_=outt[:, kt % 2, 0:1]))
                if bnd is not None and g == NG - 1:
                    p.dma("gpsimd", lambda e, kt=kt: e.dma_start(
                        out=bnd[1, kt * 128:(kt + 1) * 128].rearrange(
                            "(p one) -> p one", one=1),
                        in_=outt[:, kt % 2, NT - 1:NT]))
            return dst, post
        ln_group(prog, lnc, lambda kt: ybuf[:, kt, :], out_write)


def sec_E(prog, sb, ps, io):
    (x2r, wur, wvr, wg1r, wg2r, bu_pt, bv_pt, bg1_pt, bg2_pt, cw_pt, cb_pt,
     tor, ugr, vgr) = (io["x2r"], io["wur"], io["wvr"], io["wg1r"], io["wg2r"],
                       io["bu_pt"], io["bv_pt"], io["bg1_pt"], io["bg2_pt"],
                       io["cw_pt"], io["cb_pt"], io["tor"], io["ugr"], io["vgr"])
    stage1 = sb("aE_stage1", [128, TOK], F32)
    x2bf = sb("aE_x2bf", [128, DKT, TOK], BF16)
    tconv = sb("aE_tconv", [128, TOK], F32)
    cw = sb("aE_cw", [128, DKT, 3], F32)
    cb = sb("aE_cb", [128, DKT], F32)
    bu = sb("aE_bu", [128, DKT], F32)
    bv = sb("aE_bv", [128, DKT], F32)
    bg1 = sb("aE_bg1", [128, 4], F32)
    bg2 = sb("aE_bg2", [128, DKT], F32)
    wsl = sb("aE_wsl", [128, 8, 128], F32)
    wbf = sb("aE_wbf", [128, DKT, 128], BF16)
    hbf = sb("aE_hbf", [128, 4, TOK], BF16)
    gT = sb("aE_gT", [128, DKT, TOK], F32)
    oscr = sb("aE_oscr", [128, TOK], F32)
    hl = sb("aE_hl", [128, DKT], F32)
    hr = sb("aE_hr", [128, DKT], F32)
    pb = [ps("aE_pb%d" % i) for i in range(2)]

    prog.dma("sync", lambda e: e.dma_start(out=cw[:, :, :], in_=cw_pt[:, :, :]))
    prog.dma("sync", lambda e: e.dma_start(out=cb[:, :], in_=cb_pt[:, :]))
    prog.dma("sync", lambda e: e.dma_start(out=bu[:, :], in_=bu_pt[:, :]))
    prog.dma("sync", lambda e: e.dma_start(out=bv[:, :], in_=bv_pt[:, :]))
    prog.dma("sync", lambda e: e.dma_start(out=bg1[:, :], in_=bg1_pt[:, :]))
    prog.dma("sync", lambda e: e.dma_start(out=bg2[:, :], in_=bg2_pt[:, :]))
    prog.new_phase()
    if "halo_l_pt" in io:
        prog.dma("sync", lambda e: e.dma_start(out=hl[:, :], in_=io["halo_l_pt"]))
        prog.dma("sync", lambda e: e.dma_start(out=hr[:, :], in_=io["halo_r_pt"]))
        prog.new_phase()
    else:
        bgath = sb("aE_bgath", [16, D], F32)
        oh = sb("aE_oh", [16, 2], F32)
        prog.dma("sync", lambda e: e.dma_start(out=bgath[:, :], in_=io["bnd_g"]))
        prog.dma("sync", lambda e: e.dma_start(out=oh[:, :], in_=io["oh"]))
        prog.new_phase()
        for kt in range(DKT):
            prog.op("tensor", lambda e, kt=kt: e.matmul(
                pb[0][:, kt:kt + 1], bgath[:, kt * 128:(kt + 1) * 128], oh[:, 0:1],
                start=(kt == 0), stop=(kt == DKT - 1), skip_group_check=True))
            prog.op("tensor", lambda e, kt=kt: e.matmul(
                pb[1][:, kt:kt + 1], bgath[:, kt * 128:(kt + 1) * 128], oh[:, 1:2],
                start=(kt == 0), stop=(kt == DKT - 1), skip_group_check=True))
        prog.new_phase()
        prog.op("vector", lambda e: e.tensor_copy(out=hl[:, :], in_=pb[0][:, 0:DKT]))
        prog.op("scalar", lambda e: e.copy(out=hr[:, :], in_=pb[1][:, 0:DKT]))
        prog.new_phase()
    for kt in range(DKT):
        prog.dma("sync", lambda e, kt=kt: e.dma_start(out=stage1[:, :], in_=x2r[:, kt, :]))
        prog.new_phase()
        prog.op("scalar", lambda e, kt=kt: e.copy(out=x2bf[:, kt, :], in_=stage1[:, :]))
        prog.op("vector", lambda e, kt=kt: e.tensor_scalar_mul(
            tconv[:, :], stage1[:, :], cw[:, kt, 1:2]))
        prog.new_phase()
        prog.op("vector", lambda e, kt=kt: e.scalar_tensor_tensor(
            tconv[:, 1:], stage1[:, :TOK - 1], cw[:, kt, 0:1], tconv[:, 1:],
            ALU.mult, ALU.add))
        prog.new_phase()
        prog.op("vector", lambda e, kt=kt: e.scalar_tensor_tensor(
            tconv[:, :TOK - 1], stage1[:, 1:], cw[:, kt, 2:3], tconv[:, :TOK - 1],
            ALU.mult, ALU.add))
        prog.new_phase()
        prog.op("vector", lambda e, kt=kt: e.tensor_scalar_add(
            tconv[:, :], tconv[:, :], cb[:, kt:kt + 1]))
        prog.new_phase()
        prog.op("vector", lambda e, kt=kt: e.scalar_tensor_tensor(
            tconv[:, 0:1], hl[:, kt:kt + 1], cw[:, kt, 0:1], tconv[:, 0:1],
            ALU.mult, ALU.add))
        prog.op("vector", lambda e, kt=kt: e.scalar_tensor_tensor(
            tconv[:, TOK - 1:TOK], hr[:, kt:kt + 1], cw[:, kt, 2:3],
            tconv[:, TOK - 1:TOK], ALU.mult, ALU.add))
        prog.new_phase()
        prog.dma("gpsimd", lambda e, kt=kt: e.dma_start(out=tor[:, kt, :], in_=tconv[:, :]))
        prog.new_phase()
    for ht in range(4):
        for hh in range(2):
            prog.dma("sync", lambda e, ht=ht, hh=hh: e.dma_start(
                out=wsl[:, :, :],
                in_=wg1r[:, hh * 8:(hh + 1) * 8, ht * 128:(ht + 1) * 128]))
            prog.new_phase()
            prog.op("scalar", lambda e, hh=hh: e.copy(
                out=wbf[:, hh * 8:(hh + 1) * 8, :].rearrange("p a b -> p (a b)"),
                in_=wsl[:, :, :].rearrange("p a b -> p (a b)")))
            prog.new_phase()
        for g in range(2):
            for k in range(DKT):
                prog.op("tensor", lambda e, g=g, k=k: e.matmul(
                    pb[g][:, :], wbf[:, k, :],
                    x2bf[:, k, g * 512:(g + 1) * 512],
                    start=(k == 0), stop=(k == DKT - 1)))
        prog.new_phase()
        for g in range(2):
            prog.op("scalar", lambda e, ht=ht, g=g: e.activation(
                hbf[:, ht, g * 512:(g + 1) * 512], pb[g][:, :], AF.Relu,
                bias=bg1[:, ht:ht + 1]))
        prog.new_phase()
    for ct in range(DKT):
        prog.dma("sync", lambda e, ct=ct: e.dma_start(
            out=wsl[:, 0:4, :], in_=wg2r[:, :, ct * 128:(ct + 1) * 128]))
        prog.new_phase()
        prog.op("scalar", lambda e: e.copy(
            out=wbf[:, 0:4, :].rearrange("p a b -> p (a b)"),
            in_=wsl[:, 0:4, :].rearrange("p a b -> p (a b)")))
        prog.new_phase()
        for g in range(2):
            for k in range(4):
                prog.op("tensor", lambda e, g=g, k=k: e.matmul(
                    pb[g][:, :], wbf[:, k, :], hbf[:, k, g * 512:(g + 1) * 512],
                    start=(k == 0), stop=(k == 3)))
        prog.new_phase()
        for g in range(2):
            prog.op("scalar", lambda e, ct=ct, g=g: e.activation(
                gT[:, ct, g * 512:(g + 1) * 512], pb[g][:, :], AF.Sigmoid,
                bias=bg2[:, ct:ct + 1]))
        prog.new_phase()
    for (wr_ap, b_sb, out_r) in ((wur, bu, ugr), (wvr, bv, vgr)):
        for ct in range(DKT):
            for hh in range(2):
                prog.dma("sync", lambda e, ct=ct, hh=hh, wr_ap=wr_ap: e.dma_start(
                    out=wsl[:, :, :],
                    in_=wr_ap[:, hh * 8:(hh + 1) * 8, ct * 128:(ct + 1) * 128]))
                prog.new_phase()
                prog.op("scalar", lambda e, hh=hh: e.copy(
                    out=wbf[:, hh * 8:(hh + 1) * 8, :].rearrange("p a b -> p (a b)"),
                    in_=wsl[:, :, :].rearrange("p a b -> p (a b)")))
                prog.new_phase()
            for g in range(2):
                for k in range(DKT):
                    prog.op("tensor", lambda e, g=g, k=k: e.matmul(
                        pb[g][:, :], wbf[:, k, :],
                        x2bf[:, k, g * 512:(g + 1) * 512],
                        start=(k == 0), stop=(k == DKT - 1)))
            prog.new_phase()
            for g in range(2):
                prog.op("vector", lambda e, ct=ct, g=g, b_sb=b_sb: e.scalar_tensor_tensor(
                    oscr[:, g * 512:(g + 1) * 512], pb[g][:, :], b_sb[:, ct:ct + 1],
                    gT[:, ct, g * 512:(g + 1) * 512], ALU.add, ALU.mult))
            prog.new_phase()
            prog.dma("gpsimd", lambda e, ct=ct, out_r=out_r: e.dma_start(
                out=out_r[:, ct, :], in_=oscr[:, :]))
            prog.new_phase()


def sec_F(prog, sb, ps, io):
    ug3 = io["ug3"]
    vg3 = io["vg3"]
    go3 = io["go3"]
    u_sb = sb("aF_u", [128, T], F32)
    v_sb = sb("aF_v", [128, T], F32)
    cs = sb("aF_cs", [128, T], F32)
    v3 = lambda t: t[:, :].rearrange("p (r t2) -> p r t2", r=8)
    for h in range(2):
        prog.dma("sync", lambda e, h=h: e.dma_start(out=v3(u_sb), in_=ug3(h)))
        prog.dma("sync", lambda e, h=h: e.dma_start(out=v3(v_sb), in_=vg3(h)))
        prog.new_phase()
        prog.op("vector", lambda e: e.tensor_tensor_scan(
            cs[:, :], u_sb[:, :], u_sb[:, :], 0.0, ALU.add, ALU.bypass))
        prog.new_phase()
        prog.op("vector", lambda e: e.tensor_mul(cs[:, :], cs[:, :], v_sb[:, :]))
        prog.new_phase()
        prog.dma("gpsimd", lambda e, h=h: e.dma_start(out=go3(h), in_=v3(cs)))
        prog.new_phase()


def sec_G(prog, sb, ps, io):
    x2r, tor, gor, wur, ones_d, x3r = (io["x2r"], io["tor"], io["gor"],
                                       io["wur"], io["ones_d"], io["x3r"])
    x2sl = sb("aG_x2sl", [128, DKT, TOK], F32)
    tsl = sb("aG_tsl", [128, DKT, TOK], F32)
    gobf = sb("aG_gobf", [128, DKT, TOK], BF16)
    stage = sb("aG_stage", [128, TOK], F32)
    wsl = sb("aG_wsl", [128, 8, 128], F32)
    wbf = sb("aG_wbf", [128, DKT, 128], BF16)
    scr = sb("aG_scr", [128, 2, 512], F32)
    sq = [sb("aG_sq0", [128, NT], F32), sb("aG_sq1", [128, NT], F32)]
    stats = sb("aG_stats", [1, 4 * NT], F32)
    ones = sb("aG_ones", [128, 128], F32)
    eps = sb("aG_eps", [1, 1], F32)
    pb = [ps("aG_pb%d" % i) for i in range(6)]

    prog.op("vector", lambda e: e.memset(eps[:, :], EPS))
    for ch in range(4):
        prog.dma("sync", lambda e, ch=ch: e.dma_start(
            out=x2sl[:, ch * 4:(ch + 1) * 4, :], in_=x2r[:, ch * 4:(ch + 1) * 4, :]))
        prog.dma("sync", lambda e, ch=ch: e.dma_start(
            out=tsl[:, ch * 4:(ch + 1) * 4, :], in_=tor[:, ch * 4:(ch + 1) * 4, :]))
    prog.dma("sync", lambda e: e.dma_start(out=ones[:, :], in_=ones_d[:, :]))
    prog.new_phase()
    for kt in range(DKT):
        prog.dma("sync", lambda e, kt=kt: e.dma_start(out=stage[:, :], in_=gor[:, kt, :]))
        prog.new_phase()
        prog.op("scalar", lambda e, kt=kt: e.copy(out=gobf[:, kt, :], in_=stage[:, :]))
        prog.new_phase()
    for ct in range(DKT):
        for hh in range(2):
            prog.dma("sync", lambda e, ct=ct, hh=hh: e.dma_start(
                out=wsl[:, :, :],
                in_=wur[:, hh * 8:(hh + 1) * 8, ct * 128:(ct + 1) * 128]))
            prog.new_phase()
            prog.op("scalar", lambda e, hh=hh: e.copy(
                out=wbf[:, hh * 8:(hh + 1) * 8, :].rearrange("p a b -> p (a b)"),
                in_=wsl[:, :, :].rearrange("p a b -> p (a b)")))
            prog.new_phase()
        for g in range(2):
            for k in range(DKT):
                prog.op("tensor", lambda e, g=g, k=k: e.matmul(
                    pb[g][:, :], wbf[:, k, :], gobf[:, k, g * 512:(g + 1) * 512],
                    start=(k == 0), stop=(k == DKT - 1)))
        prog.new_phase()
        for g in range(2):
            prog.op("vector", lambda e, ct=ct, g=g: e.tensor_add(
                scr[:, g, :], pb[g][:, :], tsl[:, ct, g * 512:(g + 1) * 512]))
        prog.new_phase()
        for g in range(2):
            prog.op("gpsimd", lambda e, ct=ct, g=g: e.tensor_add(
                x2sl[:, ct, g * 512:(g + 1) * 512],
                x2sl[:, ct, g * 512:(g + 1) * 512], scr[:, g, :]))
        prog.new_phase()
    lnc = {"ones": ones, "pb_s1": pb[2], "pb_s2": pb[3], "pb_A": pb[4], "pb_B": pb[5],
           "stats": stats, "sq": sq, "eps": eps[0:1, 0:1]}
    for g in range(NG):
        def out_write(kt, g=g):
            return tsl[:, kt, g * NT:(g + 1) * NT], None
        ln_group(prog, lnc, lambda kt, g=g: x2sl[:, kt, g * NT:(g + 1) * NT], out_write)
    for ch in range(4):
        prog.dma("gpsimd", lambda e, ch=ch: e.dma_start(
            out=x3r[:, ch * 4:(ch + 1) * 4, :], in_=tsl[:, ch * 4:(ch + 1) * 4, :]))
    prog.new_phase()


def sec_H(prog, sb, ps, io):
    (x3r, su_ug, su_dp, wsc_up, wsc_gt, wsc_dp, wupr, wgtr, wdnr, h128_d,
     ones_d, x4r) = (io["x3r"], io["su_ug"], io["su_dp"], io["wsc_up"],
                     io["wsc_gt"], io["wsc_dp"], io["wupr"], io["wgtr"],
                     io["wdnr"], io["h128_d"], io["ones_d"], io["x4r"])
    C1 = 1.0 / (np.sqrt(2048.0) * 64.0)
    C2 = 1.0 / (np.sqrt(8192.0) * 64.0)
    zmid = sb("aH_zmid", [128, NFF1, NT], BF16)
    z1f = sb("aH_z1f", [128, DKT, NT], F32)
    z1bf = sb("aH_z1bf", [128, DKT, NT], BF16)
    tmpb = sb("aH_tmpb", [128, 8, NT], F32)
    stage = sb("aH_stage", [128, DKT, 128], F32)
    wbf_a = sb("aH_wbfa", [128, DKT, 128], BF16)
    wdnbf = sb("aH_wdnbf", [128, NFF1, 128], BF16)
    xs = sb("aH_xs", [128, NT], F32)
    xsbf = sb("aH_xsbf", [128, NT], BF16)
    upsb = sb("aH_upsb", [128, NT], F32)
    sgsb = sb("aH_sgsb", [128, NT], F32)
    xres = sb("aH_xres", [128, NT], F32)
    su1 = sb("aH_su1", [128, DKT], F32)
    su2 = sb("aH_su2", [128, NFF1], F32)
    w64u = sb("aH_w64u", [128, NFF1], F32)
    w64g = sb("aH_w64g", [128, NFF1], F32)
    w64d = sb("aH_w64d", [128, DKT], F32)
    h128 = sb("aH_h128", [128, 128], F32)
    h128bf = sb("aH_h128bf", [128, 128], BF16)
    sq = [sb("aH_sq0", [128, NT], F32), sb("aH_sq1", [128, NT], F32)]
    stats = sb("aH_stats", [1, 4 * NT], F32)
    ones = sb("aH_ones", [128, 128], F32)
    outt = sb("aH_outt", [128, 2, NT], F32)
    eps = sb("aH_eps", [1, 1], F32)
    pb = [ps("aH_pb%d" % i) for i in range(6)]

    prog.op("vector", lambda e: e.memset(eps[:, :], EPS))
    prog.dma("sync", lambda e: e.dma_start(out=su1[:, :], in_=su_ug))
    prog.dma("sync", lambda e: e.dma_start(out=su2[:, :], in_=su_dp))
    prog.dma("sync", lambda e: e.dma_start(out=w64u[:, :], in_=wsc_up))
    prog.dma("sync", lambda e: e.dma_start(out=w64g[:, :], in_=wsc_gt))
    prog.dma("sync", lambda e: e.dma_start(out=w64d[:, :], in_=wsc_dp))
    prog.dma("sync", lambda e: e.dma_start(out=h128[:, :], in_=h128_d))
    prog.dma("sync", lambda e: e.dma_start(out=ones[:, :], in_=ones_d))
    prog.new_phase()
    prog.op("scalar", lambda e: e.mul(w64u[:, :], w64u[:, :], 64.0))
    prog.op("vector", lambda e: e.tensor_scalar_mul(w64g[:, :], w64g[:, :], 64.0))
    prog.op("gpsimd", lambda e: e.tensor_scalar_mul(w64d[:, :], w64d[:, :], 64.0))
    prog.new_phase()
    prog.op("scalar", lambda e: e.copy(out=h128bf[:, :], in_=h128[:, :]))
    prog.new_phase()

    for g in range(NG):
        gs = slice(g * NT, (g + 1) * NT)
        for kt in range(DKT):
            prog.dma("sync", lambda e, kt=kt, gs=gs: e.dma_start(
                out=xs[:, :], in_=x3r[:, kt, gs]))
            prog.new_phase()
            prog.op("vector", lambda e, kt=kt: e.tensor_scalar_mul(
                xs[:, :], xs[:, :], su1[:, kt:kt + 1]))
            prog.new_phase()
            prog.op("scalar", lambda e: e.copy(out=xsbf[:, :], in_=xs[:, :]))
            prog.new_phase()
            prog.op("tensor", lambda e: e.matmul(
                pb[0][:, :], h128bf[:, :], xsbf[:, :], start=True, stop=True))
            prog.new_phase()
            prog.op("scalar", lambda e, kt=kt: e.activation(
                z1f[:, kt, :], pb[0][:, :], AF.Identity, scale=C1))
            prog.new_phase()
        s = 1
        while s < DKT:
            pairs = [(i, i + s) for i in range(DKT) if (i // s) % 2 == 0]
            for ci in range(0, len(pairs), 8):
                chunk = pairs[ci:ci + 8]
                for pi, (a, b) in enumerate(chunk):
                    prog.op("vector", lambda e, a=a, b=b, pi=pi: e.tensor_sub(
                        tmpb[:, pi, :], z1f[:, a, :], z1f[:, b, :]))
                prog.new_phase()
                for pi, (a, b) in enumerate(chunk):
                    prog.op("vector", lambda e, a=a, b=b: e.tensor_add(
                        z1f[:, a, :], z1f[:, a, :], z1f[:, b, :]))
                prog.new_phase()
                for pi, (a, b) in enumerate(chunk):
                    prog.op("gpsimd", lambda e, b=b, pi=pi: e.tensor_copy(
                        out=z1f[:, b, :], in_=tmpb[:, pi, :]))
                prog.new_phase()
            s *= 2
        prog.op("scalar", lambda e: e.copy(
            out=z1bf[:, :, :].rearrange("p a t -> p (a t)"),
            in_=z1f[:, :, :].rearrange("p a t -> p (a t)")))
        prog.new_phase()
        for ft in range(NFF1):
            fs = slice(ft * 128, (ft + 1) * 128)
            prog.dma("sync", lambda e, fs=fs: e.dma_start(
                out=stage[:, :, :], in_=wupr[:, :, fs]))
            prog.new_phase()
            prog.op("scalar", lambda e: e.copy(
                out=wbf_a[:, :, :].rearrange("p a b -> p (a b)"),
                in_=stage[:, :, :].rearrange("p a b -> p (a b)")))
            prog.new_phase()
            prog.dma("sync", lambda e, fs=fs: e.dma_start(
                out=stage[:, :, :], in_=wgtr[:, :, fs]))
            for k in range(DKT):
                prog.op("tensor", lambda e, k=k: e.matmul(
                    pb[0][:, :], wbf_a[:, k, :], z1bf[:, k, :],
                    start=(k == 0), stop=(k == DKT - 1)))
            prog.new_phase()
            prog.op("vector", lambda e, ft=ft: e.tensor_scalar_mul(
                upsb[:, :], pb[0][:, :], w64u[:, ft:ft + 1]))
            prog.op("scalar", lambda e: e.copy(
                out=wbf_a[:, :, :].rearrange("p a b -> p (a b)"),
                in_=stage[:, :, :].rearrange("p a b -> p (a b)")))
            prog.new_phase()
            for k in range(DKT):
                prog.op("tensor", lambda e, k=k: e.matmul(
                    pb[1][:, :], wbf_a[:, k, :], z1bf[:, k, :],
                    start=(k == 0), stop=(k == DKT - 1)))
            prog.new_phase()
            prog.op("scalar", lambda e, ft=ft: e.activation(
                sgsb[:, :], pb[1][:, :], AF.Silu, scale=w64g[:, ft:ft + 1]))
            prog.new_phase()
            prog.op("vector", lambda e, ft=ft: e.tensor_mul(
                zmid[:, ft, :], sgsb[:, :], upsb[:, :]))
            prog.new_phase()
        for ft in range(NFF1):
            prog.op("vector", lambda e, ft=ft: e.tensor_scalar_mul(
                zmid[:, ft, :], zmid[:, ft, :], su2[:, ft:ft + 1]))
        prog.new_phase()
        for ft in range(NFF1 + 1):
            if ft < NFF1:
                prog.op("tensor", lambda e, ft=ft: e.matmul(
                    pb[0 + ft % 2][:, :], h128bf[:, :], zmid[:, ft, :],
                    start=True, stop=True))
            if ft > 0:
                prog.op("scalar", lambda e, ft=ft: e.copy(
                    out=zmid[:, ft - 1, :], in_=pb[0 + (ft - 1) % 2][:, :]))
            prog.new_phase()
        s = 1
        while s < NFF1:
            pairs = [(i, i + s) for i in range(NFF1) if (i // s) % 2 == 0]
            for ci in range(0, len(pairs), 8):
                chunk = pairs[ci:ci + 8]
                for pi, (a, b) in enumerate(chunk):
                    prog.op("vector", lambda e, a=a, b=b, pi=pi: e.tensor_sub(
                        tmpb[:, pi, :], zmid[:, a, :], zmid[:, b, :]))
                prog.new_phase()
                for pi, (a, b) in enumerate(chunk):
                    prog.op("vector", lambda e, a=a, b=b: e.tensor_add(
                        zmid[:, a, :], zmid[:, a, :], zmid[:, b, :]))
                prog.new_phase()
                for pi, (a, b) in enumerate(chunk):
                    prog.op("gpsimd", lambda e, b=b, pi=pi: e.tensor_copy(
                        out=zmid[:, b, :], in_=tmpb[:, pi, :]))
                prog.new_phase()
            s *= 2
        for dt in range(DKT):
            ds = slice(dt * 128, (dt + 1) * 128)
            for ch in range(4):
                ftsl = slice(ch * 16, (ch + 1) * 16)
                prog.dma("sync", lambda e, ds=ds, ftsl=ftsl: e.dma_start(
                    out=stage[:, :, :], in_=wdnr[:, ftsl, ds]))
                prog.new_phase()
                prog.op("scalar", lambda e, ftsl=ftsl: e.copy(
                    out=wdnbf[:, ftsl, :].rearrange("p a b -> p (a b)"),
                    in_=stage[:, :, :].rearrange("p a b -> p (a b)")))
                prog.new_phase()
            prog.dma("sync", lambda e, dt=dt, gs=gs: e.dma_start(
                out=xres[:, :], in_=x3r[:, dt, gs]))
            prog.new_phase()
            for ft in range(NFF1):
                prog.op("tensor", lambda e, ft=ft: e.matmul(
                    pb[2][:, :], wdnbf[:, ft, :], zmid[:, ft, :],
                    start=(ft == 0), stop=(ft == NFF1 - 1)))
            prog.new_phase()
            prog.op("vector", lambda e, dt=dt: e.tensor_scalar(
                sgsb[:, :], pb[2][:, :], w64d[:, dt:dt + 1], C2, ALU.mult, ALU.mult))
            prog.new_phase()
            prog.op("gpsimd", lambda e, dt=dt: e.tensor_add(
                z1f[:, dt, :], sgsb[:, :], xres[:, :]))
            prog.new_phase()
        lnc = {"ones": ones, "pb_s1": pb[0], "pb_s2": pb[1], "pb_A": pb[3],
               "pb_B": pb[4], "stats": stats, "sq": sq, "eps": eps[0:1, 0:1]}

        def out_write(kt, g=g):
            dst = outt[:, kt % 2, :]

            def post(p, kt=kt, g=g):
                p.dma("gpsimd", lambda e, kt=kt, g=g: e.dma_start(
                    out=x4r[:, kt, g * NT:(g + 1) * NT], in_=outt[:, kt % 2, :]))
            return dst, post
        ln_group(prog, lnc, lambda kt: z1f[:, kt, :], out_write)


def build_FUSED():
    nc = bass.Bass("TRN2", target_bir_lowering=False)
    ein = lambda n, s: nc.dram_tensor(n, s, F32, kind="ExternalInput").ap()
    intr = lambda n, s: nc.dram_tensor(n, s, F32).ap()

    x_ext = ein("x_ext", [D, TOK + 1])
    wactT = ein("wactT", [2 * D, D])
    wrecT = ein("wrecT", [2 * D, D])
    wvalT = ein("wvalT", [D, D])
    wgT = ein("wgT", [D, FF0])
    wuT0 = ein("wuT0", [D, FF0])
    wdT = ein("wdT", [FF0, D])
    wrT = ein("wrT", [D, 8])
    agT = ein("agT", [NE * D, 16])
    auT = ein("auT", [NE * D, 16])
    bgT = ein("bgT", [NE * 16, FF0])
    buT = ein("buT", [NE * 16, FF0])
    adT = ein("adT", [NE * FF0, 16])
    bdT = ein("bdT", [NE * 16, D])
    sel_d = ein("sel", [8, 128])
    ident_d = ein("ident", [128, 128])
    ones_d = ein("ones", [128, 128])
    wuT1 = ein("wuT1", [D, D])
    wvT1 = ein("wvT1", [D, D])
    wg1T = ein("wg1T", [D, 512])
    wg2T = ein("wg2T", [512, D])
    bu_pt = ein("bu_pt", [128, DKT])
    bv_pt = ein("bv_pt", [128, DKT])
    bg1_pt = ein("bg1_pt", [128, 4])
    bg2_pt = ein("bg2_pt", [128, DKT])
    cw_pt = ein("cw_pt", [128, DKT, 3])
    cb_pt = ein("cb_pt", [128, DKT])
    oh_d = ein("oh", [16, 2])
    su_ug = ein("su_ug", [128, DKT])
    su_dp = ein("su_dp", [128, NFF1])
    wsc_up = ein("wsc_up", [128, NFF1])
    wsc_gt = ein("wsc_gt", [128, NFF1])
    wsc_dp = ein("wsc_dp", [128, DKT])
    wupT = ein("wupT", [D, FF1])
    wgtT = ein("wgtT", [D, FF1])
    wdnT = ein("wdnT", [FF1, D])
    h128_d = ein("h128", [128, 128])
    x4T = nc.dram_tensor("x4T", [D, TOK], F32, kind="ExternalOutput").ap()

    xvT = intr("i_xvT", [D, TOK])
    r4T = intr("i_r4T", [D, TOK])
    xv_rx = intr("i_xvrx", [D, TOK])
    kv_tx = intr("i_kvtx", [D, TOK])
    kv_rx = intr("i_kvrx", [D, TOK])
    x1T = intr("i_x1T", [D, TOK])
    gbT = intr("i_gbT", [FF0, TOK])
    ubT = intr("i_ubT", [FF0, TOK])
    ybT = intr("i_ybT", [D, TOK])
    x2T = intr("i_x2T", [D, TOK])
    bnd = intr("i_bnd", [2, D])
    bnd_g = intr("i_bndg", [16, D])
    toT = intr("i_toT", [D, TOK])
    ugT = intr("i_ugT", [D, TOK])
    vgT = intr("i_vgT", [D, TOK])
    ug_rx = intr("i_ugrx", [D, TOK])
    vg_rx = intr("i_vgrx", [D, TOK])
    go_tx = intr("i_gotx", [D, TOK])
    go_rx = intr("i_gorx", [D, TOK])
    x3T = intr("i_x3T", [D, TOK])

    r16 = lambda ap: ap.rearrange("(kt p) t -> p kt t", p=128)
    prs = []

    with ExitStack() as st_outer:
        F = Fused(nc, st_outer)

        def section(fn, io):
            prog = Prog()
            with ExitStack() as st:
                sb = lambda n, s, d: st.enter_context(nc.sbuf_tensor(n, s, d))
                ps = lambda n: st.enter_context(nc.psum_tensor(n, [128, 512], F32))
                fn(prog, sb, ps, io)
                F.emit_block(prog)

        def cc(kind, in_ap, out_ap):
            prog = Prog()
            prog.cc(lambda e: e.collective_compute(
                kind, ALU.bypass, replica_groups=[list(range(NCORES))],
                ins=[in_ap.opt()], outs=[out_ap.opt()]))
            prog.new_phase()
            F.emit_block(prog)

        section(sec_A, {"xr": r16(x_ext), "war": wactT.rearrange("(kt p) c -> p kt c", p=128),
                        "wrr": wrecT.rearrange("(kt p) c -> p kt c", p=128),
                        "xvr": r16(xvT), "r4r": r16(r4T)})
        cc("AllToAll", xvT, xv_rx)
        xv_v = xv_rx.rearrange("(r j two) t -> j two r t", two=2, j=128)
        kv_v = kv_tx.rearrange("(r j two) t -> j two r t", two=2, j=128)
        section(sec_B, {"re3": xv_v[:, 0, :, :], "im3": xv_v[:, 1, :, :],
                        "kvre3": kv_v[:, 0, :, :], "kvim3": kv_v[:, 1, :, :]})
        cc("AllToAll", kv_tx, kv_rx)
        section(sec_C, {"xrr": r16(x_ext)[:, :, 1:TOK + 1], "kvr": r16(kv_rx),
                        "r4r": r16(r4T), "wvr": wvalT.rearrange("(kt p) c -> p kt c", p=128),
                        "ones_d": ones_d, "x1r": r16(x1T)})
        section(sec_D1, {"x1r": r16(x1T), "wgr": wgT.rearrange("(kt p) f -> p kt f", p=128),
                         "wur": wuT0.rearrange("(kt p) f -> p kt f", p=128),
                         "wdr": wdT.rearrange("(ft p) d -> p ft d", p=128),
                         "gbr": gbT.rearrange("(ft p) t -> p ft t", p=128),
                         "ubr": ubT.rearrange("(ft p) t -> p ft t", p=128),
                         "ybr": r16(ybT)})
        section(sec_D2, {"x1r": r16(x1T), "gbr": gbT.rearrange("(ft p) t -> p ft t", p=128),
                         "ubr": ubT.rearrange("(ft p) t -> p ft t", p=128),
                         "ybr": r16(ybT), "wrr": wrT.rearrange("(kt p) m -> p kt m", p=128),
                         "agr": agT.rearrange("(ekt p) m -> p ekt m", p=128),
                         "aur": auT.rearrange("(ekt p) m -> p ekt m", p=128),
                         "bgr": bgT.rearrange("(e p) f -> p e f", p=16),
                         "bur": buT.rearrange("(e p) f -> p e f", p=16),
                         "adr": adT.rearrange("(e ft p) m -> p e ft m", p=128, ft=NFF0),
                         "bdr": bdT.rearrange("(e p) d -> p e d", p=16),
                         "sel_d": sel_d, "ident_d": ident_d, "ones_d": ones_d,
                         "x2r": r16(x2T), "bnd": bnd})
        cc("AllGather", bnd, bnd_g)
        section(sec_E, {"x2r": r16(x2T), "wur": wuT1.rearrange("(kt p) c -> p kt c", p=128),
                        "wvr": wvT1.rearrange("(kt p) c -> p kt c", p=128),
                        "wg1r": wg1T.rearrange("(kt p) c -> p kt c", p=128),
                        "wg2r": wg2T.rearrange("(kt p) c -> p kt c", p=128),
                        "bu_pt": bu_pt, "bv_pt": bv_pt, "bg1_pt": bg1_pt,
                        "bg2_pt": bg2_pt, "cw_pt": cw_pt, "cb_pt": cb_pt,
                        "bnd_g": bnd_g, "oh": oh_d,
                        "tor": r16(toT), "ugr": r16(ugT), "vgr": r16(vgT)})
        cc("AllToAll", ugT, ug_rx)
        cc("AllToAll", vgT, vg_rx)
        ug_v = ug_rx.rearrange("(r h p) t -> p h r t", h=2, p=128)
        vg_v = vg_rx.rearrange("(r h p) t -> p h r t", h=2, p=128)
        go_v = go_tx.rearrange("(r h p) t -> p h r t", h=2, p=128)
        section(sec_F, {"ug3": lambda h: ug_v[:, h, :, :],
                        "vg3": lambda h: vg_v[:, h, :, :],
                        "go3": lambda h: go_v[:, h, :, :]})
        cc("AllToAll", go_tx, go_rx)
        section(sec_G, {"x2r": r16(x2T), "tor": r16(toT), "gor": r16(go_rx),
                        "wur": wuT1.rearrange("(kt p) c -> p kt c", p=128),
                        "ones_d": ones_d, "x3r": r16(x3T)})
        section(sec_H, {"x3r": r16(x3T), "su_ug": su_ug, "su_dp": su_dp,
                        "wsc_up": wsc_up, "wsc_gt": wsc_gt, "wsc_dp": wsc_dp,
                        "wupr": wupT.rearrange("(kt p) f -> p kt f", p=128),
                        "wgtr": wgtT.rearrange("(kt p) f -> p kt f", p=128),
                        "wdnr": wdnT.rearrange("(ft p) d -> p ft d", p=128),
                        "h128_d": h128_d, "ones_d": ones_d, "x4r": r16(x4T)})
    return nc


def _ct(a):
    return np.ascontiguousarray(a)


def _pt(v):
    return _ct(v.reshape(-1, 128).T)


def _maps(inp):
    f32 = np.float32
    x = np.asarray(inp["L0_attn_x"], f32)[0]
    xT = _ct(x.T)
    ones = np.ones((128, 128), f32)
    ident = np.eye(128, dtype=f32)
    sel = np.zeros((8, 128), f32)
    for e_ in range(NE):
        sel[e_, e_ * 16:(e_ + 1) * 16] = 2.0
    h128 = np.ones((1, 1), f32)
    while h128.shape[0] < 128:
        h128 = np.block([[h128, h128], [h128, -h128]])
    h128 = _ct(h128.astype(f32))
    wsug = np.asarray(inp["L1_ffn_Wscale_ug"], f32)
    xT_pad = np.concatenate([np.zeros((D, 1), f32), xT], axis=1)

    common = {
        "wactT": _ct(np.asarray(inp["L0_attn_W_act"], f32).T),
        "wrecT": _ct(np.asarray(inp["L0_attn_W_rec"], f32).T),
        "wvalT": _ct(np.asarray(inp["L0_attn_W_val"], f32).T),
        "wgT": _ct(np.asarray(inp["L0_ffn_Wg"], f32).T),
        "wuT0": _ct(np.asarray(inp["L0_ffn_Wu"], f32).T),
        "wdT": _ct(np.asarray(inp["L0_ffn_Wd"], f32).T),
        "wrT": _ct(np.asarray(inp["L0_ffn_Wr"], f32).T),
        "agT": _ct(np.asarray(inp["L0_ffn_Ag"], f32).transpose(0, 2, 1).reshape(NE * D, 16)),
        "auT": _ct(np.asarray(inp["L0_ffn_Au"], f32).transpose(0, 2, 1).reshape(NE * D, 16)),
        "bgT": _ct(np.asarray(inp["L0_ffn_Bg"], f32).transpose(0, 2, 1).reshape(NE * 16, FF0)),
        "buT": _ct(np.asarray(inp["L0_ffn_Bu"], f32).transpose(0, 2, 1).reshape(NE * 16, FF0)),
        "adT": _ct(np.asarray(inp["L0_ffn_Ad"], f32).transpose(0, 2, 1).reshape(NE * FF0, 16)),
        "bdT": _ct(np.asarray(inp["L0_ffn_Bd"], f32).transpose(0, 2, 1).reshape(NE * 16, D)),
        "sel": sel, "ident": ident, "ones": ones,
        "wuT1": _ct(np.asarray(inp["L1_attn_Wu"], f32).T),
        "wvT1": _ct(np.asarray(inp["L1_attn_Wv"], f32).T),
        "wg1T": _ct(np.asarray(inp["L1_attn_Wg1"], f32).T),
        "wg2T": _ct(np.asarray(inp["L1_attn_Wg2"], f32).T),
        "bu_pt": _pt(np.asarray(inp["L1_attn_bu"], f32)),
        "bv_pt": _pt(np.asarray(inp["L1_attn_bv"], f32)),
        "bg1_pt": _pt(np.asarray(inp["L1_attn_bg1"], f32)),
        "bg2_pt": _pt(np.asarray(inp["L1_attn_bg2"], f32)),
        "cw_pt": _ct(np.asarray(inp["L1_attn_conv_w"], f32).reshape(16, 128, 3)
                     .transpose(1, 0, 2)),
        "cb_pt": _pt(np.asarray(inp["L1_attn_conv_b"], f32)),
        "su_ug": _pt(np.asarray(inp["L1_ffn_SU_ug"], f32)),
        "su_dp": _pt(np.asarray(inp["L1_ffn_SU_dp"], f32)),
        "wsc_up": _pt(wsug[:FF1]), "wsc_gt": _pt(wsug[FF1:]),
        "wsc_dp": _pt(np.asarray(inp["L1_ffn_Wscale_dp"], f32)),
        "wupT": _ct(np.asarray(inp["L1_ffn_W_up"], f32).T),
        "wgtT": _ct(np.asarray(inp["L1_ffn_W_gate"], f32).T),
        "wdnT": _ct(np.asarray(inp["L1_ffn_W_down"], f32).T),
        "h128": h128,
    }
    maps = []
    for c in range(NCORES):
        oh = np.zeros((16, 2), f32)
        if c > 0:
            oh[(c - 1) * 2 + 1, 0] = 1.0
        if c < NCORES - 1:
            oh[(c + 1) * 2 + 0, 1] = 1.0
        m = dict(common)
        m["x_ext"] = _ct(xT_pad[:, c * TOK:c * TOK + TOK + 1])
        m["oh"] = oh
        maps.append(m)
    return maps


def kernel(**inp):
    maps = _maps(inp)
    nc = build_FUSED()
    res = run_bass_kernel_spmd(nc, maps, core_ids=list(range(NCORES)))
    x4 = np.concatenate([r["x4T"] for r in res.results], axis=1)
    return _ct(x4.T).reshape(1, T, D)
